# version 35
# baseline (speedup 1.0000x reference)
import contextlib
import numpy as np
import concourse.bass as bass
import concourse.mybir as mybir
from concourse.bass_utils import run_bass_kernel_spmd
from concourse.alu_op_type import AluOpType as OP



F32 = mybir.dt.float32
BF16 = mybir.dt.bfloat16
U8 = mybir.dt.uint8
AF = mybir.ActivationFunctionType

C = 1024
HID = 2816
DI = 2048
DST = 16
DTR = 64
NH = 16
HD = 64
LAT = 32
MIDS = (0.25, 0.75, 1.25, 1.75, 2.5, 3.5, 5.0)
DSIZE = {F32: 4, BF16: 2}


class TE:
    __slots__ = ("w", "rs")

    def __init__(self):
        self.w = None
        self.rs = []


class KB:
    ESEM = {"pe": "e_pe", "act": "e_act", "dve": "e_dve", "pool": "e_pool",
            "sync": "e_sync"}
    QSEM = {"sync": "q_sync", "pool": "q_pool", "act": "q_act"}
    ARENA_BYTES = 140 * 1024

    def __init__(self, T=2048, parts=("a", "f1", "b", "f2")):
        self.T = T
        self.parts = parts
        self.TT = T // 128
        self.QCW = min(512, T)
        self.NQC = T // self.QCW
        nc = self.nc = bass.Bass("TRN2", target_bir_lowering=False)
        self.es = contextlib.ExitStack()
        self.prog = []
        self.sems = {}
        self.cnt = {}
        self.hw = {e: {} for e in self.ESEM}
        for s in list(self.ESEM.values()):
            self.sems[s] = self.es.enter_context(nc.semaphore(s))
            self.cnt[s] = 0
        self.qpool = {}
        self.qidx = {}
        for qeng, npool in (("sync", 24), ("pool", 16), ("act", 8)):
            names = []
            for i in range(npool):
                s = f"dq_{qeng}_{i}"
                self.sems[s] = self.es.enter_context(nc.semaphore(s))
                self.cnt[s] = 0
                names.append(s)
            self.qpool[qeng] = names
            self.qidx[qeng] = 0
        self.sems["bgq"] = self.es.enter_context(nc.semaphore("bgq"))
        self.cnt["bgq"] = 0
        self.bg = None
        self.want_wq = (("a" in parts) and
                        ("f1" in parts or "f2" in parts))
        self.dram = {}
        self._declare_dram()
        self.r = self.es.enter_context(nc.sbuf_tensor("r", [128, self.TT * C], F32))
        self.rT = [TE() for _ in range(self.TT)]
        self.arena = self.es.enter_context(
            nc.sbuf_tensor("arena", [128, self.ARENA_BYTES], U8))
        self.aoff = 0
        self.amarks = []
        self.ps = [self.es.enter_context(nc.psum_tensor(f"p{i}", [128, 512],
                                                        F32)) for i in range(8)]
        self.pse = [TE() for _ in range(8)]
        self.named = {}

    def apush(self):
        self.amarks.append(self.aoff)

    def apop(self):
        self.aoff = self.amarks.pop()

    def aview(self, n, dtype=F32):
        nb = n * DSIZE[dtype]
        nb_al = (nb + 63) & ~63
        assert self.aoff + nb_al <= self.ARENA_BYTES, \
            f"arena overflow: {self.aoff + nb_al}"
        ap = self.arena[:, self.aoff:self.aoff + nb].bitcast(dtype)
        self.aoff += nb_al
        return ap, TE()

    def buf(self, name, shape=None, dtype=F32):
        if name not in self.named:
            h = self.es.enter_context(self.nc.sbuf_tensor(name, shape, dtype))
            self.named[name] = (h, TE())
        return self.named[name]

    def _wait(self, eng, ev):
        if ev is None:
            return
        s, v = ev
        if self.hw[eng].get(s, 0) >= v:
            return
        self.hw[eng][s] = v
        sem = self.sems[s]
        self.prog.append((eng, lambda E, sem=sem, v=v: E.wait_ge(sem, v)))

    def _dep_waits(self, eng, ins, outs, dma):
        my = self.ESEM[eng]
        for t in ins:
            self._wait(eng, t.w)
        for t in outs:
            if t.w is not None and (dma or t.w[0] != my):
                self._wait(eng, t.w)
            for rv in t.rs:
                self._wait(eng, rv)

    def _commit(self, eng, ins, outs, sname, amt):
        self.cnt[sname] += amt
        ev = (sname, self.cnt[sname])
        for t in ins:
            t.rs.append(ev)
        for t in outs:
            t.w = ev
            t.rs = []
        return ev

    def op(self, eng, fn, ins=(), outs=(), dma=False):
        self._dep_waits(eng, ins, outs, dma)
        if dma:
            pool = self.qpool[eng]
            s = pool[self.qidx[eng] % len(pool)]
            self.qidx[eng] += 1
            if self.cnt[s] > 0:
                self._wait(eng, (s, self.cnt[s]))
            amt = 16
        else:
            s = self.ESEM[eng]
            amt = 1
        ev = self._commit(eng, ins, outs, s, amt)
        sem = self.sems[s]
        self.prog.append((eng, lambda E, fn=fn, sem=sem, amt=amt:
                          fn(E).then_inc(sem, amt)))
        return ev

    def mm_group(self, fns, ins=(), outs=()):
        self._dep_waits("pe", ins, outs, False)
        for fn in fns[:-1]:
            self.prog.append(("pe", fn))
        ev = self._commit("pe", ins, outs, "e_pe", 1)
        sem = self.sems["e_pe"]
        fn = fns[-1]
        self.prog.append(("pe", lambda E, fn=fn, sem=sem: fn(E).then_inc(sem, 1)))
        return ev

    def dma(self, eng, out_ap, in_ap, ins=(), outs=(), slow=False):
        return self.op(eng, lambda E: E.dma_start(
            out=out_ap, in_=in_ap, allow_slow_non_contiguous=slow),
            ins=ins, outs=outs, dma=True)

    def bgdma(self, out_ap, in_ap, outs=()):
        ev = self._commit("pool", (), outs, "bgq", 16)
        sem = self.sems["bgq"]
        self.prog.append(("pool", lambda E, out_ap=out_ap, in_ap=in_ap,
                          sem=sem: E.dma_start(
                              out=out_ap, in_=in_ap).then_inc(sem, 16)))
        return ev

    def barrier(self):
        evs = [(s, v) for s, v in self.cnt.items() if v > 0 and s != "bgq"]
        for eng in self.ESEM:
            for ev in evs:
                self._wait(eng, ev)

    def bg_step(self, n=1):
        while n > 0 and self.bg is not None:
            try:
                next(self.bg)
            except StopIteration:
                self.bg = None
            n -= 1

    def din(self, name, shape, dtype=F32):
        self.dram[name] = self.nc.dram_tensor(name, shape, dtype,
                                              kind="ExternalInput").ap()
        return self.dram[name]

    def dscr(self, name, shape, dtype):
        self.dram[name] = self.nc.dram_tensor(name, shape, dtype,
                                              kind="Internal").ap()
        return self.dram[name]

    def _declare_dram(self):
        T = self.T
        self.din("x", [T, C])
        self.y_out = self.nc.dram_tensor("y", [T, C], F32,
                                         kind="ExternalOutput").ap()
        self.din("c_attn_w", [3 * C, C])
        self.din("c_attn_b", [3 * C, 1])
        self.din("q2l_w_t", [HD, LAT])
        self.din("kv2l_w_t", [HD, LAT])
        self.din("q2l_b", [LAT, 1])
        self.din("kv2l_b", [LAT, 1])
        self.din("l2o_w_t", [LAT, HD])
        self.din("l2o_b", [HD, 1])
        self.din("wqk_w_t4", [128, 128])
        self.din("wqk_b4", [128, 1])
        self.din("cproj_w_t", [C, C])
        self.din("cproj_b", [C, 1])
        self.din("ffn_wg", [HID, C])
        self.din("ffn_wu", [HID, C])
        self.din("ffn_wd", [C, HID])
        self.din("bb_gamma", [C, 1])
        self.din("bb_beta", [C, 1])
        self.din("bb_in_w", [C, 2 * DI])
        self.din("bb_merge_w", [2 * DI, DI])
        self.din("bb_out_w", [DI, C])
        for sfx in ("", "_b"):
            self.din("bb_conv_w" + sfx, [DI, 4])
            self.din("bb_conv_b" + sfx, [DI, 1])
            self.din("bb_xproj_w" + sfx, [DI, DTR + 2 * DST])
            self.din("bb_dt_w" + sfx, [DTR, DI])
            self.din("bb_dt_b" + sfx, [DI, 1])
            self.din("bb_A_log" + sfx, [DI, DST])
            self.din("bb_D" + sfx, [DI, 1])
        self.din("ident", [128, 128])
        self.din("identb", [128, 128], BF16)
        self.din("R4_t", [128, 128])
        self.din("cos4", [128, T])
        self.din("sin4", [128, T])
        self.din("ones_row", [1, 128])
        self.din("ones_m", [128, LAT])
        self.din("kv2l_b_row", [1, LAT])
        nm = min(4, max(1, self.QCW // 128))
        self.nmask = nm
        self.din("masks", [nm * 128, self.QCW], BF16)
        NIC = (HID + 511) // 512
        self.dscr("s_wg2", [128, NIC * 4096], BF16)
        self.dscr("s_wu2", [128, NIC * 4096], BF16)
        self.dscr("s_wd2", [128, 8 * HID], BF16)
        self.dscr("s_iw", [2 * DI, C], BF16)
        self.dscr("s_mw", [DI, 2 * DI], BF16)
        self.dscr("s_ow", [C, DI], BF16)
        self.dscr("s_xs_t", [DI, T], BF16)
        self.dscr("s_sz_t", [DI, T], BF16)
        self.dscr("s_yf_t", [DI, T], BF16)
        self.dscr("s_yb_t", [DI, T], BF16)

    def precast_b(self):
        self.scr_te = {"s_iw": TE(), "s_mw": TE(), "s_ow": TE()}
        for nm, srcnm, nstripe in (("s_iw", "bb_in_w", 2 * DI // 128),
                                   ("s_mw", "bb_merge_w", DI // 128),
                                   ("s_ow", "bb_out_w", C // 128)):
            for m in range(nstripe):
                self.dma(
                    "pool",
                    self.dram[nm][m * 128:(m + 1) * 128, :].rearrange(
                        "p (a f) -> p a f", f=128),
                    self.dram[srcnm][:, m * 128:(m + 1) * 128].rearrange(
                        "(a p) f -> p a f", p=128),
                    outs=[self.scr_te[nm]])

    def build(self):
        self.load_consts()
        x_r = self.dram["x"].rearrange("(a p) f -> p a f", p=128)
        self.dma("sync", self.r[:].rearrange("p (a f) -> p a f", f=C), x_r,
                 outs=self.rT)
        if "a" in self.parts:
            self.phase_attn_a()
            self.barrier()
        if "b" in self.parts:
            self.precast_b()
        if ("f1" in self.parts or "f2" in self.parts) and not self.want_wq:
            self.apush()
            self.phase_ffn_wquant()
            self.apop()
            self.barrier()
        if "f1" in self.parts:
            self.apush()
            self.phase_ffn()
            self.apop()
            self.barrier()
        if "b" in self.parts:
            self.phase_attn_b()
            self.barrier()
        if "f2" in self.parts:
            self.apush()
            self.phase_ffn()
            self.apop()
            self.barrier()
        y_r = self.y_out.rearrange("(a p) f -> p a f", p=128)
        ev = self.dma("sync", y_r, self.r[:].rearrange("p (a f) -> p a f", f=C),
                 ins=self.rT)
        self._wait("sync", ev)
        self.finish()
        return self.nc

    def finish(self):
        with self.es:
            with self.nc.Block() as block:
                @block.sync
                def _(E):
                    for eng, fn in self.prog:
                        if eng == "sync":
                            fn(E)

                @block.tensor
                def _(E):
                    for eng, fn in self.prog:
                        if eng == "pe":
                            fn(E)

                @block.scalar
                def _(E):
                    for eng, fn in self.prog:
                        if eng == "act":
                            fn(E)

                @block.vector
                def _(E):
                    for eng, fn in self.prog:
                        if eng == "dve":
                            fn(E)

                @block.gpsimd
                def _(E):
                    for eng, fn in self.prog:
                        if eng == "pool":
                            fn(E)

    def load_consts(self):
        idb, idb_te = self.buf("identb_sb", [128, 128], BF16)
        idf, idf_te = self.buf("identf_sb", [128, 128], F32)
        ones, ones_te = self.buf("ones_sb", [1, 128], BF16)
        self.dma("sync", idb[:], self.dram["identb"], outs=[idb_te])
        self.dma("sync", idf[:], self.dram["ident"], outs=[idf_te])
        self.dma("pool", ones[:], self.dram["ones_row"], outs=[ones_te])

    def layernorm(self, src_ap, src_te, out_ap, out_te, stash=None):
        st, st_te = self.buf("ln_stats", [128, 12], F32)
        mv, mv_te = self.buf("ln_mv", [128, 2], F32)
        rs, rs_te = self.buf("ln_rs", [128, 1], F32)
        half = C // 2
        self.op("dve", lambda E: E.bn_stats(st[:, 0:6], src_ap[:, 0:half]),
                ins=[src_te], outs=[st_te])
        self.op("dve", lambda E: E.bn_stats(st[:, 6:12], src_ap[:, half:C]),
                ins=[src_te], outs=[st_te])
        self.op("dve", lambda E: E.bn_aggr(mv[:, 0:2], st[:, 0:12]),
                ins=[st_te], outs=[mv_te])
        self.op("dve", lambda E: E.tensor_scalar(rs[:], mv[:, 1:2], 1e-5, None,
                                                 op0=OP.add),
                ins=[mv_te], outs=[rs_te])
        self.op("dve", lambda E: E.reciprocal(rs[:], rs[:]),
                ins=[rs_te], outs=[rs_te])
        self.op("act", lambda E: E.activation(rs[:], rs[:], AF.Sqrt),
                ins=[rs_te], outs=[rs_te])
        self.op("dve", lambda E: E.tensor_scalar(out_ap, src_ap, mv[:, 0:1],
                                                 rs[:, 0:1], op0=OP.subtract,
                                                 op1=OP.mult),
                ins=[src_te, mv_te, rs_te], outs=[out_te])
        if stash is not None:
            sa, sa_te = stash
            self.op("dve", lambda E: E.tensor_copy(sa[:, 0:1], mv[:, 0:1]),
                    ins=[mv_te], outs=[sa_te])
            self.op("dve", lambda E: E.tensor_copy(sa[:, 1:2], rs[:, 0:1]),
                    ins=[rs_te], outs=[sa_te])

    QCH = 1408

    def quant(self, x_ap, x_te, W, out_ap, out_te, qb):
        for _ in self.quant_gen(x_ap, x_te, W, out_ap, out_te, qb):
            pass

    def quant_gen(self, x_ap, x_te, W, out_ap, out_te, qb):
        for c0 in range(0, W, self.QCH):
            cw = min(self.QCH, W - c0)
            yield from self._quant_chunk(x_ap[:, c0:c0 + cw], x_te,
                                         out_ap[:, c0:c0 + cw], out_te, cw, qb)

    def _quant_chunk(self, x_ap, x_te, out_ap, out_te, W, qb):
        nb = W // 16
        (zt, zt_te), (ix, ix_te) = qb["t1"], qb["ix"]
        (bm, bm_te), (sc, sc_te), (rsc, rsc_te) = qb["bm"], qb["sc"], qb["rsc"]
        sch, sch_te = qb["sch"]
        xb = x_ap.rearrange("p (b e) -> p b e", e=16)
        self.op("dve", lambda E: E.tensor_reduce(
            bm[:, 0:nb], xb, axis=mybir.AxisListType.X, op=OP.max,
            apply_absolute_value=True), ins=[x_te], outs=[bm_te])
        yield
        C6A = float(np.float32(1.0 / 6.0))
        C6B = float(np.float32(1.0 / 6.0 - np.float32(1.0 / 6.0)))
        self.op("dve", lambda E: E.tensor_scalar(
            rsc[:, 0:nb], bm[:, 0:nb], C6A, None, op0=OP.mult),
            ins=[bm_te], outs=[rsc_te])
        self.op("dve", lambda E: E.scalar_tensor_tensor(
            sc[:, 0:nb], in0=bm[:, 0:nb], scalar=C6B, in1=rsc[:, 0:nb],
            op0=OP.mult, op1=OP.add), ins=[bm_te, rsc_te], outs=[sc_te])
        self.op("dve", lambda E: E.tensor_scalar(
            sc[:, 0:nb], sc[:, 0:nb], 1e-12, None, op0=OP.max),
            ins=[sc_te], outs=[sc_te])
        self.op("dve", lambda E: E.tensor_scalar(
            sch[:, 0:nb], sc[:, 0:nb], 0.5, None, op0=OP.mult),
            ins=[sc_te], outs=[sch_te])
        self.op("dve", lambda E: E.reciprocal(rsc[:, 0:nb], sc[:, 0:nb]),
                ins=[sc_te], outs=[rsc_te])
        yield
        rscb = rsc[:, 0:nb, None].broadcast_to([128, nb, 16])
        self.op("dve", lambda E: E.tensor_tensor(
            zt[:, 0:W].rearrange("p (b e) -> p b e", e=16), xb, rscb,
            op=OP.mult), ins=[x_te, rsc_te], outs=[zt_te])
        yield
        sg, sg_te = qb["sg"]
        self.op("dve", lambda E: E.tensor_scalar(
            sg[:, 0:W], x_ap, 0.0, None, op0=OP.is_gt),
            ins=[x_te], outs=[sg_te])
        yield
        self.op("dve", lambda E: E.tensor_scalar(
            sg[:, 0:W], sg[:, 0:W], 2.0, -1.0, op0=OP.mult, op1=OP.add),
            ins=[sg_te], outs=[sg_te])
        yield
        self.op("dve", lambda E: E.tensor_tensor(
            zt[:, 0:W], zt[:, 0:W], zt[:, 0:W], op=OP.mult),
            ins=[zt_te], outs=[zt_te])
        yield
        self.op("dve", lambda E: E.tensor_scalar(
            ix[:, 0:W], zt[:, 0:W], 0.0625, None, op0=OP.is_gt),
            ins=[zt_te], outs=[ix_te])
        yield
        for thr in (0.5625, 1.5625, 3.0625):
            self.op("dve", lambda E, thr=thr: E.scalar_tensor_tensor(
                ix[:, 0:W], in0=zt[:, 0:W], scalar=thr, in1=ix[:, 0:W],
                op0=OP.is_gt, op1=OP.add), ins=[zt_te, ix_te], outs=[ix_te])
            yield
        hi, hi_te = qb["hi"]
        self.op("dve", lambda E: E.tensor_scalar(
            hi[:, 0:W], zt[:, 0:W], 25.0, None, op0=OP.is_gt),
            ins=[zt_te], outs=[hi_te])
        yield
        for thr in (25.0, 12.25, 6.25):
            self.op("dve", lambda E, thr=thr: E.scalar_tensor_tensor(
                hi[:, 0:W], in0=zt[:, 0:W], scalar=thr, in1=hi[:, 0:W],
                op0=OP.is_gt, op1=OP.add), ins=[zt_te, hi_te], outs=[hi_te])
            yield
        self.op("dve", lambda E: E.scalar_tensor_tensor(
            ix[:, 0:W], in0=hi[:, 0:W], scalar=2.0, in1=ix[:, 0:W],
            op0=OP.mult, op1=OP.add), ins=[hi_te, ix_te], outs=[ix_te])
        yield
        self.op("dve", lambda E: E.tensor_tensor(
            sg[:, 0:W], ix[:, 0:W], sg[:, 0:W], op=OP.mult),
            ins=[ix_te, sg_te], outs=[sg_te])
        yield
        schb = sch[:, 0:nb, None].broadcast_to([128, nb, 16])
        self.op("dve", lambda E: E.tensor_tensor(
            out_ap.rearrange("p (b e) -> p b e", e=16),
            sg[:, 0:W].rearrange("p (b e) -> p b e", e=16), schb, op=OP.mult),
            ins=[sg_te, sch_te], outs=[out_te])
        yield

    def mk_qbufs(self, W):
        ch = min(W, self.QCH)
        cb = ch // 16
        return {"ix": self.aview(ch, BF16),
                "t1": self.aview(ch), "sg": self.aview(ch, BF16),
                "hi": self.aview(ch, BF16),
                "bm": self.aview(cb), "sc": self.aview(cb),
                "rsc": self.aview(cb), "sch": self.aview(cb)}

    def transpose128(self, src_ap, src_te, dst_ap, dst_te, pbank, bf=True,
                     add_to=False, width=128):
        idb, idb_te = self.buf("identb_sb", [128, 128], BF16)
        idf, idf_te = self.buf("identf_sb", [128, 128], F32)
        ident, ident_te = (idb, idb_te) if bf else (idf, idf_te)
        if bf:
            p = self.ps[pbank][:, 0:(width + 1) // 2].bitcast(BF16)
        else:
            p = self.ps[pbank][:, 0:width]
        pte = self.pse[pbank]
        self.op("pe", lambda E: E.transpose(p, src_ap, ident[:]),
                ins=[src_te, ident_te], outs=[pte])
        if not add_to:
            self.op("act", lambda E: E.copy(dst_ap, p),
                    ins=[pte], outs=[dst_te])
        else:
            self.op("dve", lambda E: E.tensor_tensor(dst_ap, dst_ap,
                                                     p, op=OP.add),
                    ins=[pte, dst_te], outs=[dst_te])

    def transpose_addr(self, src_ap, src_te, dst_ap, dst_te, pbank):
        idf, idf_te = self.buf("identf_sb", [128, 128], F32)
        p = self.ps[pbank][:, 0:128]
        pte = self.pse[pbank]
        fns = [lambda E: E.matmul(p, src_ap, idf[:], start=True, stop=False),
               lambda E: E.matmul(p, idf[:], dst_ap, start=False, stop=True)]
        self.mm_group(fns, ins=[src_te, dst_te, idf_te], outs=[pte])
        self.op("act", lambda E: E.copy(dst_ap, p), ins=[pte], outs=[dst_te])

    def phase_ffn_wquant(self):
        for _ in self.wquant_gen():
            pass

    def wquant_gen(self):
        av = self.aview
        win, win_te = av(1408, F32)
        wq, wq_te = av(1408, BF16)
        wstg, wstg_te = av(1408, BF16)
        qb = self.mk_qbufs(1408)
        yield
        for wname, scr in (("ffn_wg", "s_wg2"), ("ffn_wu", "s_wu2")):
            for i in range(HID // 128):
                self.dma("sync", win[:, 0:C],
                         self.dram[wname][i * 128:(i + 1) * 128, :],
                         ins=[win_te], outs=[win_te])
                yield
                yield from self.quant_gen(win[:, 0:C], win_te, C,
                                          wq[:, 0:C], wq_te, qb)
                ic, fo2 = i // 4, (i % 4) * 128
                for ft in range(C // 128):
                    self.transpose128(wq[:, ft * 128:(ft + 1) * 128], wq_te,
                                      wstg[:, ft * 128:(ft + 1) * 128],
                                      wstg_te, pbank=7)
                    yield
                for ft in range(C // 128):
                    o = ic * 4096 + ft * 512 + fo2
                    self.dma("sync", self.dram[scr][:, o:o + 128],
                             wstg[:, ft * 128:(ft + 1) * 128], ins=[wstg_te])
                yield
        for i in range(C // 128):
            for h0 in range(0, HID, 1408):
                hw = min(1408, HID - h0)
                self.dma("sync", win[:, 0:hw],
                         self.dram["ffn_wd"][i * 128:(i + 1) * 128, h0:h0 + hw],
                         ins=[win_te], outs=[win_te])
                yield
                yield from self.quant_gen(win[:, 0:hw], win_te, hw,
                                          wq[:, 0:hw], wq_te, qb)
                for jt in range(hw // 128):
                    self.transpose128(wq[:, jt * 128:(jt + 1) * 128], wq_te,
                                      wstg[:, jt * 128:(jt + 1) * 128],
                                      wstg_te, pbank=7)
                    yield
                self.dma("sync",
                         self.dram["s_wd2"][:, i * HID + h0:
                                            i * HID + h0 + hw],
                         wstg[:, 0:hw], ins=[wstg_te])
                yield

    def phase_ffn(self):
        TT = self.TT
        NIC = (HID + 511) // 512
        ics = [(i * 512, min(512, HID - i * 512)) for i in range(NIC)]
        qb = self.mk_qbufs(HID)
        ln, ln_te = self.aview(C, F32)
        xqp = [self.aview(C, BF16) for _ in range(2)]
        xqT = [[self.aview(C, BF16) for _ in range(2)] for _ in range(2)]
        h = [self.aview(HID, F32) for _ in range(2)]
        hsg = [self.aview(HID, BF16) for _ in range(2)]
        hqp = [self.aview(HID, BF16) for _ in range(2)]
        hqT = [self.aview(HID, BF16) for _ in range(2)]
        wgs = [self.aview(8 * 512, BF16) for _ in range(2)]
        wus = [self.aview(8 * 512, BF16) for _ in range(2)]
        wds = [self.aview(HID, BF16) for _ in range(2)]
        ybp = [self.aview(128, F32) for _ in range(2)]
        groups = [( [g0, g0 + 1] if g0 + 1 < TT else [g0] )
                  for g0 in range(0, TT, 2)]

        def s1q(gidx):
            for gi, tt in enumerate(groups[gidx]):
                rsl = self.r[:, tt * C:(tt + 1) * C]
                self.layernorm(rsl, self.rT[tt], ln[:, 0:C], ln_te)
                xq, xq_te = xqp[gi]
                self.quant(ln[:, 0:C], ln_te, C, xq[:, 0:C], xq_te, qb)

        def s1T(gidx):
            for gi, tt in enumerate(groups[gidx]):
                xq, xq_te = xqp[gi]
                xT, xT_te = xqT[gidx % 2][gi]
                for ft in range(C // 128):
                    self.transpose128(xq[:, ft * 128:(ft + 1) * 128], xq_te,
                                      xT[:, ft * 128:(ft + 1) * 128], xT_te,
                                      pbank=6 + (ft % 2))

        def s2_load(ici):
            ic0, icw = ics[ici]
            wg, wg_te = wgs[ici % 2]
            wu, wu_te = wus[ici % 2]
            if icw == 512:
                self.dma("sync", wg[:, 0:8 * 512],
                         self.dram["s_wg2"][:, ici * 4096:(ici + 1) * 4096],
                         outs=[wg_te])
                self.dma("sync", wu[:, 0:8 * 512],
                         self.dram["s_wu2"][:, ici * 4096:(ici + 1) * 4096],
                         outs=[wu_te])
            else:
                self.dma("sync",
                         wg[:, 0:8 * 512].rearrange(
                             "p (a b) -> p a b", b=512)[:, :, 0:icw],
                         self.dram["s_wg2"][:, ici * 4096:(ici + 1) * 4096]
                             .rearrange("p (a b) -> p a b", b=512)[:, :, 0:icw],
                         outs=[wg_te])
                self.dma("sync",
                         wu[:, 0:8 * 512].rearrange(
                             "p (a b) -> p a b", b=512)[:, :, 0:icw],
                         self.dram["s_wu2"][:, ici * 4096:(ici + 1) * 4096]
                             .rearrange("p (a b) -> p a b", b=512)[:, :, 0:icw],
                         outs=[wu_te])

        def s2(gidx):
            gts = groups[gidx]
            s2_load(0)
            for ici, (ic0, icw) in enumerate(ics):
                wg, wg_te = wgs[ici % 2]
                wu, wu_te = wus[ici % 2]
                if ici + 1 < len(ics):
                    s2_load(ici + 1)
                for gi, tt in enumerate(gts):
                    xT, xT_te = xqT[gidx % 2][gi]
                    pg, pg_te = self.ps[2 * gi], self.pse[2 * gi]
                    pu, pu_te = self.ps[2 * gi + 1], self.pse[2 * gi + 1]
                    fns = []
                    for k in range(C // 128):
                        st = (k == 0)
                        sp = (k == C // 128 - 1)
                        fns.append(lambda E, k=k, st=st, sp=sp, xT=xT, icw=icw,
                                   pg=pg, wg=wg:
                                   E.matmul(pg[:, 0:icw],
                                            xT[:, k * 128:(k + 1) * 128],
                                            wg[:, k * 512:k * 512 + icw],
                                            start=st, stop=sp))
                        fns.append(lambda E, k=k, st=st, sp=sp, xT=xT, icw=icw,
                                   pu=pu, wu=wu:
                                   E.matmul(pu[:, 0:icw],
                                            xT[:, k * 128:(k + 1) * 128],
                                            wu[:, k * 512:k * 512 + icw],
                                            start=st, stop=sp))
                    self.mm_group(fns, ins=[xT_te, wg_te, wu_te],
                                  outs=[pg_te, pu_te])
                    hs, hs_te = hsg[gi]
                    hb, hb_te = h[gi]
                    self.op("act", lambda E, ic0=ic0, icw=icw, pg=pg, hs=hs:
                            E.activation(hs[:, ic0:ic0 + icw], pg[:, 0:icw],
                                         AF.Silu),
                            ins=[pg_te], outs=[hs_te])
                    self.op("act", lambda E, ic0=ic0, icw=icw, pu=pu, hb=hb:
                            E.copy(hb[:, ic0:ic0 + icw], pu[:, 0:icw]),
                            ins=[pu_te], outs=[hb_te])

        def s2mul(gidx):
            for gi in range(len(groups[gidx])):
                hs, hs_te = hsg[gi]
                hb, hb_te = h[gi]
                self.op("dve", lambda E, hs=hs, hb=hb: E.tensor_tensor(
                    hb[:, 0:HID], hs[:, 0:HID], hb[:, 0:HID], op=OP.mult),
                    ins=[hs_te, hb_te], outs=[hb_te])

        def s3q(gidx):
            for gi in range(len(groups[gidx])):
                hb, hb_te = h[gi]
                hq, hq_te = hqp[gi]
                self.quant(hb[:, 0:HID], hb_te, HID, hq[:, 0:HID], hq_te, qb)

        def s3T(gidx):
            for gi in range(len(groups[gidx])):
                hq, hq_te = hqp[gi]
                hT, hT_te = hqT[gi]
                for jt in range(HID // 128):
                    self.transpose128(hq[:, jt * 128:(jt + 1) * 128], hq_te,
                                      hT[:, jt * 128:(jt + 1) * 128], hT_te,
                                      pbank=6 + (jt % 2))

        def s4(gidx):
            gts = groups[gidx]
            for fo in range(C // 128):
                wd, wd_te = wds[fo % 2]
                self.dma("sync", wd[:, 0:HID],
                         self.dram["s_wd2"][:, fo * HID:(fo + 1) * HID],
                         outs=[wd_te])
                for k in range(HID // 128):
                    for gi in range(len(gts)):
                        hT, hT_te = hqT[gi]
                        st = (k == 0)
                        sp = (k == HID // 128 - 1)
                        self.op("pe", lambda E, gi=gi, k=k, st=st, sp=sp,
                                wd=wd, hT=hT:
                                E.matmul(self.ps[4 + gi][:, 0:128],
                                         wd[:, k * 128:(k + 1) * 128],
                                         hT[:, k * 128:(k + 1) * 128],
                                         start=st, stop=sp),
                                ins=[wd_te, hT_te], outs=[self.pse[4 + gi]])
                for gi, tt in enumerate(gts):
                    yb, yb_te = ybp[gi % 2]
                    self.op("act", lambda E, gi=gi, yb=yb: E.copy(
                        yb[:, 0:128], self.ps[4 + gi][:, 0:128]),
                        ins=[self.pse[4 + gi]], outs=[yb_te])
                    rsl = self.r[:, tt * C + fo * 128: tt * C + (fo + 1) * 128]
                    self.transpose_addr(yb[:, 0:128], yb_te, rsl, self.rT[tt],
                                        pbank=6 + (gi % 2))

        ng = len(groups)
        s1q(0)
        s1T(0)
        if ng > 1:
            s1q(1)
            s1T(1)
        s2(0)
        s2mul(0)
        for g in range(ng):
            s3q(g)
            if g + 1 < ng:
                s2(g + 1)
                s2mul(g + 1)
            if g + 2 < ng:
                s1q(g + 2)
            s3T(g)
            s4(g)
            if g + 2 < ng:
                s1T(g + 2)

    def phase_attn_a(self):
        T, TT, QCW, NQC = self.T, self.TT, self.QCW, self.NQC
        KPC = QCW // 128
        NKT = T // 128
        av = self.aview
        self.apush()
        mask_sb = [av(QCW, BF16) for _ in range(self.nmask)]
        qfkT = [av(T, BF16) for _ in range(4)]
        khaT = [av(T, BF16) for _ in range(4)]
        vsb = [av(T, BF16) for _ in range(4)]
        cbias, cbias_te = av(16, F32)
        l2o4, l2o4_te = av(64, BF16)
        l2ob2, l2ob2_te = av(1, F32)
        onesm, onesm_te = av(LAT, BF16)
        cprojb, cprojb_te = av(8, F32)
        if self.want_wq:
            self.bg = self.wquant_gen()
            self.bg_step(1)
        self.apush()
        xnT, xnT_te = av(8 * T, BF16)
        cosb, cosb_te = av(T, BF16)
        sinb, sinb_te = av(T, BF16)
        R4b, R4b_te = av(128, BF16)
        wqk4b, wqk4b_te = av(128, BF16)
        mrg, mrg_te = av(3 * 8 * 128, BF16)
        cab, cab_te = av(48, BF16)
        latb = {}
        for nm2 in ("q2l_b", "kv2l_b"):
            latb[nm2] = av(1, F32)
        kvrow, kvrow_te = av(LAT, F32)
        pq2l, pq2l_te = av(LAT, BF16)
        pkv2l, pkv2l_te = av(LAT, BF16)
        mbq, mbq_te = av(1, F32)
        mbk, mbk_te = av(1, F32)
        vbr, vbr_te = av(128, BF16)
        sqb, sqb_te = av(T, BF16)
        crow, crow_te = av(16, BF16)
        m1, m1_te = av(16, F32)
        qlsp = [av(512, BF16) for _ in range(2)]
        qhat, qhat_te = av(512, BF16)
        tmp1, tmp1_te = av(512, BF16)
        tmp2, tmp2_te = av(512, BF16)
        cwb = [av(1024, BF16)]
        self.dma("pool", cosb, self.dram["cos4"], outs=[cosb_te])
        self.dma("pool", sinb, self.dram["sin4"], outs=[sinb_te])
        for mi in range(self.nmask):
            m, mte = mask_sb[mi]
            self.dma("sync", m, self.dram["masks"][mi * 128:(mi + 1) * 128, :],
                     outs=[mte])
        self.dma("pool", R4b, self.dram["R4_t"], outs=[R4b_te])
        self.dma("pool", wqk4b, self.dram["wqk_w_t4"], outs=[wqk4b_te])
        self.dma("pool", cab[0:64, 0:48],
                 self.dram["c_attn_b"].rearrange("(a p) o -> p (a o)", p=64),
                 outs=[cab_te], slow=True)
        for blk in range(4):
            self.dma("sync", latb["q2l_b"][0][32 * blk:32 * blk + 32, 0:1],
                     self.dram["q2l_b"], outs=[latb["q2l_b"][1]])
            self.dma("sync", latb["kv2l_b"][0][32 * blk:32 * blk + 32, 0:1],
                     self.dram["kv2l_b"], outs=[latb["kv2l_b"][1]])
            self.dma("pool", l2o4[32 * blk:32 * blk + 32, 0:64],
                     self.dram["l2o_w_t"], outs=[l2o4_te])
        self.dma("pool", pq2l[0:64, 0:LAT], self.dram["q2l_w_t"], outs=[pq2l_te])
        self.dma("pool", pkv2l[0:64, 0:LAT], self.dram["kv2l_w_t"], outs=[pkv2l_te])
        self.dma("sync", kvrow[0:1, 0:LAT], self.dram["kv2l_b_row"], outs=[kvrow_te])
        for half in range(2):
            self.dma("sync", l2ob2[64 * half:64 * half + 64, 0:1],
                     self.dram["l2o_b"], outs=[l2ob2_te])
        self.dma("pool", onesm, self.dram["ones_m"], outs=[onesm_te])
        ones_sb, ones_te = self.buf("ones_sb", [1, 128], BF16)
        self.dma("sync", cprojb[0:128, 0:8],
                 self.dram["cproj_b"].rearrange("(a p) o -> p (a o)", p=128),
                 outs=[cprojb_te], slow=True)
        ln, ln_te = av(C, BF16)
        for tt in range(TT):
            rsl = self.r[:, tt * C:(tt + 1) * C]
            self.layernorm(rsl, self.rT[tt], ln[:, 0:C], ln_te)
            for ft in range(8):
                self.transpose128(ln[:, ft * 128:(ft + 1) * 128], ln_te,
                                  xnT[:, ft * T + tt * 128: ft * T + tt * 128 + 128],
                                  xnT_te, pbank=6 + (ft % 2))
        PR = {"q": 0, "k": 1, "v": 2}
        for g in range(4):
            self.op("dve", lambda E: E.memset(vbr[0:1, 0:128], 1.0),
                    outs=[vbr_te])
            for proj, pi in PR.items():
                pw, pw_te = (pq2l, pq2l_te) if proj == "q" else (pkv2l, pkv2l_te)
                wrow = {"q": 0, "k": 1, "v": 2}[proj]
                for hh in range(4):
                    self.bg_step(4)
                    h = 4 * g + hh
                    cw, cw_te = cwb[0]
                    self.dma("pool", cw[0:64, 0:C],
                             self.dram["c_attn_w"][wrow * C + h * HD: wrow * C + h * HD + HD, :],
                             outs=[cw_te])
                    fns = []
                    for ft in range(8):
                        fns.append(lambda E, ft=ft, cw=cw, pw=pw:
                                   E.matmul(self.ps[0][:, ft * 32:(ft + 1) * 32],
                                            cw[0:64, ft * 128:(ft + 1) * 128],
                                            pw[0:64, 0:LAT], start=True, stop=True))
                    self.mm_group(fns, ins=[cw_te, pw_te], outs=[self.pse[0]])
                    for ft in range(8):
                        self.op("act", lambda E, ft=ft, pi=pi, hh=hh:
                                E.copy(mrg[:, (pi * 8 + ft) * 128 + hh * 32:
                                           (pi * 8 + ft) * 128 + hh * 32 + 32],
                                       self.ps[0][:, ft * 32:(ft + 1) * 32]),
                                ins=[self.pse[0]], outs=[mrg_te])
                    bq_ap = cab[0:64, wrow * 16 + h: wrow * 16 + h + 1]
                    if proj in ("q", "k"):
                        self.op("pe", lambda E, hh=hh, pw=pw, bq_ap=bq_ap:
                                E.matmul(self.ps[1][32 * hh:32 * hh + 32, 0:1],
                                         pw[0:64, 0:LAT], bq_ap, start=True,
                                         stop=True, tile_position=(0, 32 * hh)),
                                ins=[pw_te, cab_te], outs=[self.pse[1]])
                        mb, mb_te = (mbq, mbq_te) if proj == "q" else (mbk, mbk_te)
                        lb, lb_te = latb["q2l_b" if proj == "q" else "kv2l_b"]
                        self.op("dve", lambda E, hh=hh, mb=mb, lb=lb:
                                E.tensor_tensor(mb[32 * hh:32 * hh + 32, 0:1],
                                                self.ps[1][32 * hh:32 * hh + 32, 0:1],
                                                lb[32 * hh:32 * hh + 32, 0:1],
                                                op=OP.add),
                                ins=[self.pse[1], lb_te], outs=[mb_te])
                    else:
                        self.op("pe", lambda E, bq_ap=bq_ap, pw=pw:
                                E.matmul(self.ps[1][0:1, 0:LAT], bq_ap,
                                         pw[0:64, 0:LAT], start=True, stop=True),
                                ins=[pw_te, cab_te], outs=[self.pse[1]])
                        self.op("dve", lambda E, hh=hh:
                                E.tensor_tensor(vbr[0:1, hh * 32:hh * 32 + 32],
                                                self.ps[1][0:1, 0:LAT],
                                                kvrow[0:1, 0:LAT], op=OP.add),
                                ins=[self.pse[1], kvrow_te], outs=[vbr_te])
            qfk, qfk_te = qfkT[g]
            kha, kha_te = khaT[g]
            vg, vg_te = vsb[g]
            for tc in range(NQC):
                t0 = tc * QCW
                qls, qls_te = qlsp[tc % 2]
                for proj in ("q", "k"):
                    self.bg_step(4)
                    pi = PR[proj]
                    fns = []
                    for hh in range(4):
                        for k in range(8):
                            fns.append(lambda E, k=k, hh=hh, pi=pi, t0=t0:
                                       E.matmul(self.ps[0][32 * hh:32 * hh + 32, 0:QCW],
                                                mrg[:, (pi * 8 + k) * 128 + hh * 32:
                                                    (pi * 8 + k) * 128 + hh * 32 + 32],
                                                xnT[:, k * T + t0: k * T + t0 + QCW],
                                                start=(k == 0), stop=(k == 7),
                                                tile_position=(0, 32 * hh)))
                    self.mm_group(fns, ins=[mrg_te, xnT_te], outs=[self.pse[0]])
                    mb, mb_te = (mbq, mbq_te) if proj == "q" else (mbk, mbk_te)
                    self.op("act", lambda E, mb=mb, qls=qls: E.activation(
                        qls[:, 0:QCW], self.ps[0][:, 0:QCW], AF.Identity,
                        bias=mb[:, 0:1]), ins=[self.pse[0], mb_te], outs=[qls_te])
                    self.op("pe", lambda E, qls=qls: E.matmul(
                        self.ps[1][:, 0:QCW], R4b, qls[:, 0:QCW],
                        start=True, stop=True), ins=[R4b_te, qls_te],
                        outs=[self.pse[1]])
                    self.op("dve", lambda E, t0=t0: E.tensor_tensor(
                        tmp1[:, 0:QCW], self.ps[1][:, 0:QCW],
                        sinb[:, t0:t0 + QCW], op=OP.mult),
                        ins=[self.pse[1], sinb_te], outs=[tmp1_te])
                    self.op("dve", lambda E, t0=t0, qls=qls: E.tensor_tensor(
                        tmp2[:, 0:QCW], qls[:, 0:QCW], cosb[:, t0:t0 + QCW],
                        op=OP.mult), ins=[qls_te, cosb_te], outs=[tmp2_te])
                    if proj == "k":
                        self.op("dve", lambda E, t0=t0, kha=kha: E.tensor_tensor(
                            kha[:, t0:t0 + QCW], tmp1[:, 0:QCW], tmp2[:, 0:QCW],
                            op=OP.add), ins=[tmp1_te, tmp2_te], outs=[kha_te])
                    else:
                        self.op("dve", lambda E: E.tensor_tensor(
                            qhat[:, 0:QCW], tmp1[:, 0:QCW], tmp2[:, 0:QCW],
                            op=OP.add), ins=[tmp1_te, tmp2_te], outs=[qhat_te])
                        self.op("pe", lambda E: E.matmul(
                            self.ps[2][:, 0:QCW], wqk4b, qhat[:, 0:QCW],
                            start=True, stop=True), ins=[wqk4b_te, qhat_te],
                            outs=[self.pse[2]])
                        wb, wb_te = self.buf("wqkb4_sb", [128, 1], F32)
                        if wb_te.w is None:
                            self.dma("sync", wb[:], self.dram["wqk_b4"],
                                     outs=[wb_te])
                        self.op("act", lambda E, t0=t0, qfk=qfk: E.activation(
                            qfk[:, t0:t0 + QCW], self.ps[2][:, 0:QCW],
                            AF.Identity, bias=wb[:, 0:1]),
                            ins=[self.pse[2], wb_te], outs=[qfk_te])
                for s128 in range(QCW // 128):
                    self.bg_step(2)
                    tt128 = (t0 + s128 * 128) // 128
                    fns = []
                    for k in range(8):
                        fns.append(lambda E, k=k, tt128=tt128:
                                   E.matmul(self.ps[3][:, 0:128],
                                            xnT[:, k * T + tt128 * 128:
                                                k * T + tt128 * 128 + 128],
                                            mrg[:, (2 * 8 + k) * 128:
                                                (2 * 8 + k) * 128 + 128],
                                            start=(k == 0), stop=False))
                    fns.append(lambda E: E.matmul(
                        self.ps[3][:, 0:128], ones_sb[0:1, 0:128],
                        vbr[0:1, 0:128], start=False, stop=True))
                    self.mm_group(fns, ins=[mrg_te, xnT_te, vbr_te, ones_te],
                                  outs=[self.pse[3]])
                    self.op("act", lambda E, tt128=tt128, vg=vg: E.copy(
                        vg[:, tt128 * 128:(tt128 + 1) * 128],
                        self.ps[3][:, 0:128]), ins=[self.pse[3]], outs=[vg_te])
            for src_, src_te_, which in ((qfk, qfk_te, 0), (kha, kha_te, 1)):
                self.op("dve", lambda E, src_=src_: E.tensor_tensor(
                    sqb[:, 0:T], src_[:, 0:T], src_[:, 0:T], op=OP.mult),
                    ins=[src_te_], outs=[sqb_te])
                nch = (T + 511) // 512
                for hh in range(4):
                    self.bg_step(2)
                    h = 4 * g + hh
                    for ch in range(nch):
                        c0 = ch * 512
                        cw_ = min(512, T - c0)
                        self.op("pe", lambda E, hh=hh, c0=c0, cw_=cw_:
                                E.matmul(self.ps[hh][0:1, 0:cw_],
                                         onesm[32 * hh:32 * hh + 32, 0:1],
                                         sqb[32 * hh:32 * hh + 32, c0:c0 + cw_],
                                         start=True, stop=True,
                                         tile_position=(32 * hh, 0)),
                                ins=[sqb_te, onesm_te], outs=[self.pse[hh]])
                        self.op("dve", lambda E, hh=hh, ch=ch, cw_=cw_:
                                E.tensor_reduce(m1[0:1, ch:ch + 1],
                                                self.ps[hh][0:1, 0:cw_],
                                                axis=mybir.AxisListType.X,
                                                op=OP.max),
                                ins=[self.pse[hh]], outs=[m1_te])
                    self.op("dve", lambda E, nch=nch: E.tensor_reduce(
                        m1[0:1, 8:9], m1[0:1, 0:nch], axis=mybir.AxisListType.X,
                        op=OP.max), ins=[m1_te], outs=[m1_te])
                    self.op("act", lambda E: E.activation(
                        m1[0:1, 8:9], m1[0:1, 8:9], AF.Sqrt),
                        ins=[m1_te], outs=[m1_te])
                    if which == 0:
                        self.op("dve", lambda E, h=h: E.tensor_copy(
                            crow[0:1, h:h + 1], m1[0:1, 8:9]),
                            ins=[m1_te], outs=[crow_te])
                    else:
                        self.op("dve", lambda E, h=h: E.scalar_tensor_tensor(
                            crow[0:1, h:h + 1], in0=m1[0:1, 8:9],
                            scalar=-float(LAT) ** -0.5, in1=crow[0:1, h:h + 1],
                            op0=OP.mult, op1=OP.mult),
                            ins=[m1_te, crow_te], outs=[crow_te])
        self.op("pe", lambda E: E.matmul(self.ps[0][:, 0:16], ones_sb[0:1, 0:128],
                                         crow[0:1, 0:16], start=True, stop=True),
                ins=[crow_te, ones_te], outs=[self.pse[0]])
        self.op("act", lambda E: E.copy(cbias[:, 0:16], self.ps[0][:, 0:16]),
                ins=[self.pse[0]], outs=[cbias_te])
        self.apop()
        self.barrier()
        self.apush()
        attb = [av(QCW, BF16) for _ in range(8)]
        riv = [av(512, F32) for _ in range(2)]
        ylat = [av(512, BF16) for _ in range(2)]
        yT, yT_te = av(8 * 512, BF16)
        Lbp = [av(512, BF16) for _ in range(2)]
        cpw = [av(1024, BF16) for _ in range(2)]
        SC = float(LAT) ** -0.5
        for qc in range(NQC):
            q0 = qc * QCW
            ktmax = qc * KPC + KPC - 1
            for g in range(4):
                qfk, qfk_te = qfkT[g]
                kha, kha_te = khaT[g]
                vg, vg_te = vsb[g]
                rv, rv_te = riv[g % 2]
                yl, yl_te = ylat[g % 2]
                bav = 4 + (g % 2)
                bsm = 6
                def emit_qk(kt):
                    for j in range(4):
                        self.op("pe", lambda E, j=j, kt=kt, q0=q0, kha=kha,
                                qfk=qfk:
                                E.matmul(self.ps[j][:, 0:QCW],
                                         kha[32 * j:32 * j + 32,
                                             kt * 128:kt * 128 + 128],
                                         qfk[32 * j:32 * j + 32, q0:q0 + QCW],
                                         start=True, stop=True,
                                         tile_position=(32 * j, 0)),
                                ins=[kha_te, qfk_te], outs=[self.pse[j]])

                emit_qk(0)
                for kt in range(ktmax + 1):
                    self.bg_step(7)
                    mi = kt - qc * KPC
                    par = 4 * (kt % 2)
                    st_ = (kt == 0)
                    sp_ = (kt == ktmax)
                    for j in range(4):
                        ab, ab_te = attb[par + j]
                        h = 4 * g + j
                        self.op("act", lambda E, j=j, h=h, ab=ab: E.activation(
                            ab[:, 0:QCW], self.ps[j][:, 0:QCW], AF.Exp,
                            scale=SC, bias=cbias[:, h:h + 1]),
                            ins=[self.pse[j], cbias_te], outs=[ab_te])
                        if mi >= 0:
                            mk, mk_te = mask_sb[mi]
                            self.op("dve", lambda E, ab=ab, mk=mk:
                                    E.tensor_tensor(ab[:, 0:QCW], ab[:, 0:QCW],
                                                    mk[:, 0:QCW], op=OP.mult),
                                    ins=[mk_te, ab_te], outs=[ab_te])
                    if kt < ktmax:
                        emit_qk(kt + 1)
                    fns = []
                    for j in range(4):
                        ab, ab_te = attb[par + j]
                        fns.append(lambda E, j=j, kt=kt, ab=ab, vg=vg, bav=bav,
                                   st_=st_, sp_=sp_:
                                   E.matmul(self.ps[bav][32 * j:32 * j + 32, 0:QCW],
                                            vg[:, kt * 128 + 32 * j:
                                               kt * 128 + 32 * j + 32],
                                            ab[:, 0:QCW], start=st_, stop=sp_,
                                            tile_position=(0, 32 * j),
                                            skip_group_check=True))
                    self.mm_group(fns, ins=[attb[par][1], attb[par + 1][1],
                                            attb[par + 2][1], attb[par + 3][1],
                                            vg_te],
                                  outs=[self.pse[bav]])
                    fns = []
                    for j in range(4):
                        ab, ab_te = attb[par + j]
                        fns.append(lambda E, j=j, ab=ab, bsm=bsm,
                                   st_=st_, sp_=sp_:
                                   E.matmul(self.ps[bsm][32 * j:32 * j + 32, 0:QCW],
                                            onesm[:, 0:LAT], ab[:, 0:QCW],
                                            start=st_, stop=sp_,
                                            tile_position=(0, 32 * j),
                                            skip_group_check=True))
                    self.mm_group(fns, ins=[attb[par][1], attb[par + 1][1],
                                            attb[par + 2][1], attb[par + 3][1],
                                            onesm_te],
                                  outs=[self.pse[bsm]])
                self.bg_step(3)
                self.op("dve", lambda E, bsm=bsm, rv=rv: E.reciprocal(
                    rv[:, 0:QCW], self.ps[bsm][:, 0:QCW]),
                    ins=[self.pse[bsm]], outs=[rv_te])
                self.op("dve", lambda E, bav=bav, rv=rv, yl=yl: E.tensor_tensor(
                    yl[:, 0:QCW], self.ps[bav][:, 0:QCW], rv[:, 0:QCW],
                    op=OP.mult), ins=[self.pse[bav], rv_te], outs=[yl_te])
                for j in range(4):
                    h = 4 * g + j
                    bank = j // 2
                    hoff = 64 * (j % 2)
                    self.op("pe", lambda E, j=j, bank=bank, hoff=hoff, yl=yl:
                            E.matmul(self.ps[bank][hoff:hoff + 64, 0:QCW],
                                     l2o4[32 * j:32 * j + 32, 0:64],
                                     yl[32 * j:32 * j + 32, 0:QCW],
                                     start=True, stop=True,
                                     tile_position=(32 * j, hoff)),
                            ins=[yl_te, l2o4_te], outs=[self.pse[bank]])
                for bank in range(2):
                    pt = 2 * g + bank
                    self.op("act", lambda E, bank=bank, pt=pt: E.activation(
                        yT[:, pt * 512: pt * 512 + QCW], self.ps[bank][:, 0:QCW],
                        AF.Identity, bias=l2ob2[:, 0:1]),
                        ins=[self.pse[bank], l2ob2_te], outs=[yT_te])
            for fo in range(8):
                self.bg_step(2)
                bank = fo % 2
                cp, cp_te = cpw[fo % 2]
                Lb, Lb_te = Lbp[fo % 2]
                self.dma("pool",
                         cp[:, 0:1024].rearrange("p (a f) -> p a f", f=128),
                         self.dram["cproj_w_t"][:, fo * 128:(fo + 1) * 128].rearrange(
                             "(a p) f -> p a f", p=128), outs=[cp_te])
                for fi in range(8):
                    self.op("pe", lambda E, bank=bank, fi=fi, cp=cp:
                            E.matmul(self.ps[bank][:, 0:QCW],
                                     cp[:, fi * 128:(fi + 1) * 128],
                                     yT[:, fi * 512: fi * 512 + QCW],
                                     start=(fi == 0), stop=(fi == 7)),
                            ins=[cp_te, yT_te], outs=[self.pse[bank]])
                self.op("act", lambda E, bank=bank, fo=fo, Lb=Lb: E.activation(
                    Lb[:, 0:QCW], self.ps[bank][:, 0:QCW], AF.Identity,
                    bias=cprojb[:, fo:fo + 1]),
                    ins=[self.pse[bank], cprojb_te], outs=[Lb_te])
                for s128 in range(QCW // 128):
                    tt = (q0 + s128 * 128) // 128
                    rsl = self.r[:, tt * C + fo * 128: tt * C + (fo + 1) * 128]
                    self.transpose128(Lb[:, s128 * 128:(s128 + 1) * 128], Lb_te,
                                      rsl, self.rT[tt], pbank=7,
                                      add_to=True)
        while self.bg is not None:
            self.bg_step(64)
        self.apop()
        self.apop()

    def phase_attn_b(self):
        T, TT = self.T, self.TT
        SEGW = min(512, T)
        NSEG = T // SEGW
        NDT = DI // 128
        av = self.aview
        self.apush()
        msb, msb_te = av(2 * TT, F32)
        self.apush()
        ln1, ln1_te = av(C, F32)
        ln2, ln2_te = av(C, BF16)
        ln2T, ln2T_te = av(8 * T, BF16)
        gcol, gcol_te = av(8, F32)
        bcol, bcol_te = av(8, BF16)
        bprime, bprime_te = av(32, F32)
        wiw = [av(C, BF16) for _ in range(2)]
        stgf = [av(512, BF16) for _ in range(2)]
        stgb = [av(512, BF16) for _ in range(2)]
        self.dma("sync", gcol[:, 0:8],
                 self.dram["bb_gamma"].rearrange("(a p) o -> p (a o)", p=128),
                 outs=[gcol_te], slow=True)
        self.dma("pool", bcol[:, 0:8],
                 self.dram["bb_beta"].rearrange("(a p) o -> p (a o)", p=128),
                 outs=[bcol_te], slow=True)
        for tt in range(TT):
            rsl = self.r[:, tt * C:(tt + 1) * C]
            self.layernorm(rsl, self.rT[tt], ln1[:, 0:C], ln1_te,
                           stash=(msb[:, 2 * tt:2 * tt + 2], msb_te))
            self.layernorm(ln1[:, 0:C], ln1_te, ln2[:, 0:C], ln2_te)
            for ft in range(8):
                p = self.ps[6 + (ft % 2)][:, 0:64].bitcast(BF16)
                pte = self.pse[6 + (ft % 2)]
                idb, idb_te = self.buf("identb_sb", [128, 128], BF16)
                self.op("pe", lambda E, ft=ft, p=p: E.transpose(
                    p, ln2[:, ft * 128:(ft + 1) * 128], idb[:]),
                    ins=[ln2_te, idb_te], outs=[pte])
                self.op("act", lambda E, ft=ft, tt=tt, p=p: E.activation(
                    ln2T[:, ft * T + tt * 128: ft * T + tt * 128 + 128],
                    p, AF.Identity, scale=gcol[:, ft:ft + 1]),
                    ins=[pte, gcol_te], outs=[ln2T_te])
            self.op("dve", lambda E, rsl=rsl: E.tensor_tensor(
                rsl, rsl, ln1[:, 0:C], op=OP.add),
                ins=[ln1_te], outs=[self.rT[tt]])
        tw = min(512, T)
        for m in range(32):
            wt, wt_te = wiw[m % 2]
            self.dma("sync", wt[:, 0:C],
                     self.dram["s_iw"][m * 128:(m + 1) * 128, :],
                     ins=[wt_te, self.scr_te["s_iw"]], outs=[wt_te])
            fns = []
            for k in range(8):
                fns.append(lambda E, k=k, m=m, wt=wt: E.matmul(
                    self.ps[2][:, m:m + 1], wt[:, k * 128:(k + 1) * 128],
                    bcol[:, k:k + 1], start=(k == 0), stop=(k == 7)))
            self.mm_group(fns, ins=[wt_te, bcol_te], outs=[self.pse[2]])
            self.op("act", lambda E, m=m: E.copy(
                bprime[:, m:m + 1], self.ps[2][:, m:m + 1]),
                ins=[self.pse[2]], outs=[bprime_te])
            for tc in range(T // tw):
                t0 = tc * tw
                bank = tc % 2
                fns = []
                for k in range(8):
                    fns.append(lambda E, bank=bank, k=k, t0=t0, tw=tw, wt=wt:
                               E.matmul(self.ps[bank][:, 0:tw],
                                        wt[:, k * 128:(k + 1) * 128],
                                        ln2T[:, k * T + t0: k * T + t0 + tw],
                                        start=(k == 0), stop=(k == 7)))
                self.mm_group(fns, ins=[wt_te, ln2T_te], outs=[self.pse[bank]])
                if m < 16:
                    sg_, sg_te_ = stgf[tc % 2]
                    self.op("act", lambda E, bank=bank, m=m, tw=tw, sg_=sg_:
                            E.activation(sg_[:, 0:tw], self.ps[bank][:, 0:tw],
                                         AF.Identity, bias=bprime[:, m:m + 1]),
                            ins=[self.pse[bank], bprime_te], outs=[sg_te_])
                    self.dma("sync", self.dram["s_xs_t"][m * 128:(m + 1) * 128,
                                                         t0:t0 + tw],
                             sg_[:, 0:tw], ins=[sg_te_])
                else:
                    sg_, sg_te_ = stgb[tc % 2]
                    self.op("act", lambda E, bank=bank, m=m, tw=tw, sg_=sg_:
                            E.activation(sg_[:, 0:tw], self.ps[bank][:, 0:tw],
                                         AF.Silu, bias=bprime[:, m:m + 1]),
                            ins=[self.pse[bank], bprime_te], outs=[sg_te_])
                    self.dma("sync", self.dram["s_sz_t"][(m - 16) * 128:
                                                         (m - 15) * 128,
                                                         t0:t0 + tw],
                             sg_[:, 0:tw], ins=[sg_te_])
        self.apop()
        self.barrier()
        for rev, sfx, yscr in ((True, "_b", "s_yb_t"), (False, "", "s_yf_t")):
            self.apush()
            asb, asb_te = av(16 * DST, F32)
            dtbc, dtbc_te = av(16, F32)
            cwc, cwc_te = av(16 * 4, F32)
            cbc, cbc_te = av(16, F32)
            dskc, dskc_te = av(16, F32)
            xpw, xpw_te = av(16 * 96, BF16)
            dtw, dtw_te = av(DI, BF16)
            astg, astg_te = av(DST, F32)
            self.dma("sync", dtbc[:, 0:16],
                     self.dram["bb_dt_b" + sfx].rearrange("(a p) o -> p (a o)", p=128),
                     outs=[dtbc_te], slow=True)
            self.dma("sync", cwc[:, 0:64].rearrange("p (a k) -> p a k", k=4),
                     self.dram["bb_conv_w" + sfx].rearrange("(a p) k -> p a k", p=128),
                     outs=[cwc_te], slow=True)
            self.dma("sync", cbc[:, 0:16],
                     self.dram["bb_conv_b" + sfx].rearrange("(a p) o -> p (a o)", p=128),
                     outs=[cbc_te], slow=True)
            self.dma("sync", dskc[:, 0:16],
                     self.dram["bb_D" + sfx].rearrange("(a p) o -> p (a o)", p=128),
                     outs=[dskc_te], slow=True)
            self.dma("pool", dtw[0:64, 0:DI], self.dram["bb_dt_w" + sfx],
                     outs=[dtw_te])
            for dt in range(NDT):
                self.dma("pool", xpw[:, dt * 96:(dt + 1) * 96],
                         self.dram["bb_xproj_w" + sfx][dt * 128:(dt + 1) * 128, :],
                         outs=[xpw_te])
                self.dma("sync", astg[:, 0:DST],
                         self.dram["bb_A_log" + sfx][dt * 128:(dt + 1) * 128, :],
                         ins=[astg_te], outs=[astg_te])
                self.op("act", lambda E: E.activation(astg[:, 0:DST],
                                                      astg[:, 0:DST], AF.Exp),
                        ins=[astg_te], outs=[astg_te])
                self.op("dve", lambda E, dt=dt: E.tensor_scalar(
                    asb[:, dt * DST:(dt + 1) * DST], astg[:, 0:DST], -1.0, None,
                    op0=OP.mult), ins=[astg_te], outs=[asb_te])
            dgb, dgb_te = av(64 * 128, BF16)
            idb_, idb_te_ = self.buf("identb_sb", [128, 128], BF16)
            for dt in range(NDT):
                for k in range(4):
                    self.op("dve", lambda E, dt=dt, k=k: E.tensor_scalar(
                        dgb[:, (dt * 4 + k) * 128:(dt * 4 + k + 1) * 128],
                        idb_[:], cwc[:, dt * 4 + k:dt * 4 + k + 1], None,
                        op0=OP.mult), ins=[idb_te_, cwc_te], outs=[dgb_te])
            xcb, xcb_te = av(16 * SEGW, BF16)
            brep, brep_te = av(16 * SEGW, BF16)
            crep, crep_te = av(16 * SEGW, BF16)
            dtin, dtin_te = av(SEGW, BF16)
            bcf, bcf_te = av(SEGW, BF16)

            dttp = [av(SEGW, BF16) for _ in range(2)]
            dtt_tes = [TE() for _ in range(NDT)]
            xss = [av(SEGW + 16, BF16) for _ in range(3)]
            spxp = [av(SEGW, F32) for _ in range(2)]
            sptp = [av(SEGW, F32) for _ in range(2)]
            slabA, _ = av(16 * SEGW, BF16)
            slabA_tes = [TE() for _ in range(DST)]
            bflat = slabA
            slabBp = []
            for _p in range(2):
                ap_, _ = av(16 * SEGW, BF16)
                slabBp.append((ap_, [TE() for _ in range(DST)]))
            cflat = slabBp[1][0]
            du, du_te = av(SEGW, BF16)
            carry, carry_te = av(16 * DST, F32)
            ystgp = [av(SEGW, BF16) for _ in range(2)]
            segs = list(range(NSEG))
            if rev:
                segs = segs[::-1]
            for si, seg in enumerate(segs):
                t0 = seg * SEGW
                first = (si == 0)
                def load_xss(dt):
                    xs_, xs_te = xss[dt % 3]
                    if not rev:
                        lo = t0 - 3
                        if lo < 0:
                            self.op("dve", lambda E, xs_=xs_: E.memset(
                                xs_[:, 0:3], 0.0), outs=[xs_te])
                            self.dma("sync", xs_[:, 3:3 + SEGW],
                                     self.dram["s_xs_t"][dt * 128:(dt + 1) * 128,
                                                         0:SEGW],
                                     ins=[xs_te], outs=[xs_te])
                        else:
                            self.dma("sync", xs_[:, 0:SEGW + 3],
                                     self.dram["s_xs_t"][dt * 128:(dt + 1) * 128,
                                                         lo:t0 + SEGW],
                                     ins=[xs_te], outs=[xs_te])
                        taps = [(k, k) for k in range(4)]
                    else:
                        hi = t0 + SEGW + 3
                        if hi > T:
                            self.op("dve", lambda E, xs_=xs_: E.memset(
                                xs_[:, SEGW:SEGW + 3], 0.0), outs=[xs_te])
                            self.dma("sync", xs_[:, 0:SEGW],
                                     self.dram["s_xs_t"][dt * 128:(dt + 1) * 128,
                                                         t0:T],
                                     ins=[xs_te], outs=[xs_te])
                        else:
                            self.dma("sync", xs_[:, 0:SEGW + 3],
                                     self.dram["s_xs_t"][dt * 128:(dt + 1) * 128,
                                                         t0:hi],
                                     ins=[xs_te], outs=[xs_te])
                        taps = [(k, 3 - k) for k in range(4)]
                    return taps

                for dt in range(2):
                    load_xss(dt)
                for dt in range(NDT):
                    xs_, xs_te = xss[dt % 3]
                    taps = load_xss(dt) if False else None
                    if not rev:
                        taps = [(k, k) for k in range(4)]
                    else:
                        taps = [(k, 3 - k) for k in range(4)]
                    if dt + 2 < NDT:
                        load_xss(dt + 2)
                    cb_ = 1 + 2 * (dt % 2)
                    fns = []
                    for ki, (k, sh) in enumerate(taps):
                        fns.append(lambda E, k=k, sh=sh, dt=dt, xs_=xs_,
                                   cb_=cb_, st=(ki == 0), sp=(ki == 3):
                                   E.matmul(self.ps[cb_][:, 0:SEGW],
                                            dgb[:, (dt * 4 + k) * 128:
                                                (dt * 4 + k + 1) * 128],
                                            xs_[:, sh:sh + SEGW],
                                            start=st, stop=sp))
                    self.mm_group(fns, ins=[dgb_te, xs_te],
                                  outs=[self.pse[cb_]])
                    self.op("act", lambda E, dt=dt, cb_=cb_: E.activation(
                        xcb[:, dt * SEGW:(dt + 1) * SEGW],
                        self.ps[cb_][:, 0:SEGW],
                        AF.Silu, bias=cbc[:, dt:dt + 1]),
                        ins=[self.pse[cb_], cbc_te], outs=[xcb_te])
                    self.op("pe", lambda E, dt=dt: E.matmul(
                        self.ps[0][0:96, 0:SEGW], xpw[:, dt * 96:(dt + 1) * 96],
                        xcb[:, dt * SEGW:(dt + 1) * SEGW],
                        start=(dt == 0), stop=(dt == NDT - 1)),
                        ins=[xpw_te, xcb_te], outs=[self.pse[0]])
                self.op("act", lambda E: E.copy(dtin[0:64, 0:SEGW],
                                                self.ps[0][0:64, 0:SEGW]),
                        ins=[self.pse[0]], outs=[dtin_te])
                self.op("act", lambda E: E.copy(bcf[64:96, 0:SEGW],
                                                self.ps[0][64:96, 0:SEGW]),
                        ins=[self.pse[0]], outs=[bcf_te])
                self.dma("sync",
                         bflat[0:1, 0:16 * SEGW].rearrange("o (a f) -> o a f", a=16),
                         bcf[64:80, 0:SEGW], ins=[bcf_te], outs=slabA_tes)
                self.dma("sync",
                         cflat[0:1, 0:16 * SEGW].rearrange("o (a f) -> o a f", a=16),
                         bcf[80:96, 0:SEGW], ins=[bcf_te], outs=slabBp[1][1])
                ones_sb, ones_te = self.buf("ones_sb", [1, 128], BF16)
                NCH = 16 * SEGW // 512
                for ch in range(NCH):
                    nsl = list(range(ch * 512 // SEGW,
                                     (ch * 512 + 511) // SEGW + 1))
                    self.op("pe", lambda E, ch=ch: E.matmul(
                        self.ps[2][:, 0:512], ones_sb[0:1, 0:128],
                        bflat[0:1, ch * 512:(ch + 1) * 512],
                        start=True, stop=True),
                        ins=[slabA_tes[n] for n in nsl] + [ones_te],
                        outs=[self.pse[2]])
                    self.op("dve", lambda E, ch=ch: E.tensor_copy(
                        brep[:, ch * 512:(ch + 1) * 512], self.ps[2][:, 0:512]),
                        ins=[self.pse[2]], outs=[brep_te])
                    self.op("pe", lambda E, ch=ch: E.matmul(
                        self.ps[3][:, 0:512], ones_sb[0:1, 0:128],
                        cflat[0:1, ch * 512:(ch + 1) * 512],
                        start=True, stop=True),
                        ins=[slabBp[1][1][n] for n in nsl] + [ones_te],
                        outs=[self.pse[3]])
                    self.op("act", lambda E, ch=ch: E.copy(
                        crep[:, ch * 512:(ch + 1) * 512], self.ps[3][:, 0:512]),
                        ins=[self.pse[3]], outs=[crep_te])
                def emit_hc(dt):
                    slb, slb_tes = slabBp[dt % 2]
                    self.op("dve", lambda E, slb=slb: E.tensor_tensor(
                        slb[:, 0:DST * SEGW], slb[:, 0:DST * SEGW],
                        crep[:, 0:DST * SEGW], op=OP.mult),
                        ins=slb_tes + [crep_te], outs=slb_tes)

                def emit_tail(dt, t0=t0):
                    slb, slb_tes = slabBp[dt % 2]
                    w = DST * SEGW // 2
                    while w >= SEGW:
                        self.op("dve", lambda E, w=w, slb=slb: E.tensor_tensor(
                            slb[:, 0:w], slb[:, 0:w], slb[:, w:2 * w],
                            op=OP.add), ins=slb_tes, outs=slb_tes)
                        w //= 2
                    ystg, ystg_te = ystgp[dt % 2]
                    self.op("dve", lambda E, dt=dt, ystg=ystg, slb=slb:
                            E.scalar_tensor_tensor(
                        ystg[:, 0:SEGW], in0=xcb[:, dt * SEGW:(dt + 1) * SEGW],
                        scalar=dskc[:, dt:dt + 1], in1=slb[:, 0:SEGW],
                        op0=OP.mult, op1=OP.add),
                        ins=[xcb_te, dskc_te] + slb_tes, outs=[ystg_te])
                    self.dma("sync", self.dram[yscr][dt * 128:(dt + 1) * 128,
                                                     t0:t0 + SEGW],
                             ystg[:, 0:SEGW], ins=[ystg_te])

                onec, onec_te = self.buf("onec_sb", [128, 1], F32)
                if onec_te.w is None:
                    self.op("dve", lambda E: E.memset(onec[:, 0:1], 1.0),
                            outs=[onec_te])
                for dt in range(NDT):
                    slb, slb_tes = slabBp[dt % 2]
                    bdt = 1 + (dt % 2)
                    self.op("pe", lambda E, dt=dt, bdt=bdt: E.matmul(
                        self.ps[bdt][:, 0:SEGW],
                        dtw[0:64, dt * 128:(dt + 1) * 128],
                        dtin[0:64, 0:SEGW], start=True, stop=True),
                        ins=[dtw_te, dtin_te], outs=[self.pse[bdt]])
                    spx, spx_te = spxp[dt % 2]
                    spt, spt_te = sptp[dt % 2]
                    self.op("act", lambda E, dt=dt, spx=spx, bdt=bdt:
                            E.activation(spx[:, 0:SEGW], self.ps[bdt][:, 0:SEGW],
                                         AF.Identity, bias=dtbc[:, dt:dt + 1]),
                            ins=[self.pse[bdt], dtbc_te], outs=[spx_te])
                    self.op("act", lambda E, spt=spt, spx=spx: E.activation(
                        spt[:, 0:SEGW], spx[:, 0:SEGW], AF.Abs),
                        ins=[spx_te], outs=[spt_te])
                    self.op("act", lambda E, spt=spt: E.activation(
                        spt[:, 0:SEGW], spt[:, 0:SEGW], AF.Exp, scale=-1.0),
                        ins=[spt_te], outs=[spt_te])
                    self.op("act", lambda E, spt=spt, onec=onec: E.activation(
                        spt[:, 0:SEGW], spt[:, 0:SEGW], AF.Ln,
                        bias=onec[:, 0:1]),
                        ins=[spt_te, onec_te], outs=[spt_te])
                    dts = dttp[dt % 2][0][:, 0:SEGW]
                    self.op("dve", lambda E, dts=dts, spx=spx, spt=spt:
                            E.scalar_tensor_tensor(
                        dts, in0=spx[:, 0:SEGW], scalar=0.0, in1=spt[:, 0:SEGW],
                        op0=OP.max, op1=OP.add),
                        ins=[spx_te, spt_te], outs=[dtt_tes[dt]])
                    self.op("dve", lambda E, dt=dt, dts=dts: E.tensor_tensor(
                        du[:, 0:SEGW], dts, xcb[:, dt * SEGW:(dt + 1) * SEGW],
                        op=OP.mult), ins=[dtt_tes[dt], xcb_te], outs=[du_te])
                    for n in range(DST):
                        self.op("act", lambda E, dt=dt, n=n, dts=dts:
                                E.activation(slabA[:, n * SEGW:(n + 1) * SEGW],
                                             dts, AF.Exp,
                                             scale=asb[:, dt * DST + n:
                                                       dt * DST + n + 1]),
                                ins=[dtt_tes[dt], asb_te], outs=[slabA_tes[n]])
                    dub = du[:, None, 0:SEGW].broadcast_to([128, DST, SEGW])
                    self.op("dve", lambda E, dub=dub, slb=slb: E.tensor_tensor(
                        slb[:, 0:DST * SEGW].rearrange("p (n f) -> p n f", n=DST),
                        dub,
                        brep[:, 0:DST * SEGW].rearrange("p (n f) -> p n f", n=DST),
                        op=OP.mult), ins=[du_te, brep_te], outs=slb_tes)
                    for n in range(DST):
                        sl = slice(n * SEGW, (n + 1) * SEGW)
                        if rev:
                            aps = (slb[:, sl][:, ::-1], slabA[:, sl][:, ::-1],
                                   slb[:, sl][:, ::-1])
                        else:
                            aps = (slb[:, sl], slabA[:, sl], slb[:, sl])
                        self.op("dve", lambda E, n=n, dt=dt, first=first,
                                aps=aps:
                                E.tensor_tensor_scan(
                                    aps[0], aps[1], aps[2],
                                    0.0 if first else
                                    carry[:, dt * DST + n:dt * DST + n + 1],
                                    op0=OP.mult, op1=OP.add),
                                ins=[slabA_tes[n], slb_tes[n], carry_te],
                                outs=[slb_tes[n]])
                    endc = 0 if rev else SEGW - 1
                    hv = slb[:, 0:DST * SEGW].rearrange(
                        "p (n f) -> p n f", n=DST)[:, :, endc:endc + 1]
                    self.op("dve", lambda E, dt=dt, hv=hv:
                            E.tensor_copy(carry[:, dt * DST:(dt + 1) * DST], hv),
                            ins=slb_tes, outs=[carry_te])
                    if dt > 0:
                        emit_tail(dt - 1)
                    emit_hc(dt)
                emit_tail(NDT - 1)
            self.apop()
            self.barrier()
        self.apush()
        TW2 = min(1024, T)
        HW_ = min(512, TW2)
        NH = TW2 // HW_
        yfb = [av(TW2, BF16) for _ in range(NDT)]
        ybb = [av(TW2, BF16) for _ in range(NDT)]
        mo = [av(TW2, BF16) for _ in range(NDT)]
        szbp = [av(TW2, BF16) for _ in range(2)]
        mwb = [av(2 * DI, BF16) for _ in range(2)]
        owb = [av(DI, BF16) for _ in range(2)]
        ostg = [av(TW2, F32) for _ in range(2)]
        yins = [b[1] for b in yfb] + [b[1] for b in ybb]
        for tc in range(T // TW2):
            t0 = tc * TW2
            for k in range(NDT):
                yf, yf_te = yfb[k]
                yb, yb_te = ybb[k]
                self.dma("sync", yf[:, 0:TW2],
                         self.dram["s_yf_t"][k * 128:(k + 1) * 128, t0:t0 + TW2],
                         ins=[yf_te], outs=[yf_te])
                self.dma("sync", yb[:, 0:TW2],
                         self.dram["s_yb_t"][k * 128:(k + 1) * 128, t0:t0 + TW2],
                         ins=[yb_te], outs=[yb_te])
            for m in range(NDT):
                pb = 2 * (m % 2)
                wt, wt_te = mwb[m % 2]
                self.dma("sync", wt[:, 0:2 * DI],
                         self.dram["s_mw"][m * 128:(m + 1) * 128, :],
                         ins=[wt_te, self.scr_te["s_mw"]], outs=[wt_te])
                szb, szb_te = szbp[m % 2]
                self.dma("pool", szb[:, 0:TW2],
                         self.dram["s_sz_t"][m * 128:(m + 1) * 128, t0:t0 + TW2],
                         ins=[szb_te], outs=[szb_te])
                fns = []
                for k in range(2 * NDT):
                    ysrc, _ = yfb[k] if k < NDT else ybb[k - NDT]
                    for hh in range(NH):
                        fns.append(lambda E, k=k, hh=hh, pb=pb, wt=wt,
                                   ysrc=ysrc:
                                   E.matmul(self.ps[pb + hh][:, 0:HW_],
                                            wt[:, k * 128:(k + 1) * 128],
                                            ysrc[:, hh * HW_:(hh + 1) * HW_],
                                            start=(k == 0),
                                            stop=(k == 2 * NDT - 1)))
                self.mm_group(fns, ins=[wt_te] + yins,
                              outs=[self.pse[pb + hh2] for hh2 in range(NH)])
                mob, mob_te = mo[m]
                for hh in range(NH):
                    self.op("dve", lambda E, hh=hh, pb=pb, mob=mob, szb=szb:
                            E.tensor_tensor(mob[:, hh * HW_:(hh + 1) * HW_],
                                            self.ps[pb + hh][:, 0:HW_],
                                            szb[:, hh * HW_:(hh + 1) * HW_],
                                            op=OP.mult),
                            ins=[self.pse[pb + hh], szb_te], outs=[mob_te])
            moins = [b[1] for b in mo]
            for fo in range(8):
                wt, wt_te = owb[fo % 2]
                self.dma("sync", wt[:, 0:DI],
                         self.dram["s_ow"][fo * 128:(fo + 1) * 128, :],
                         ins=[wt_te, self.scr_te["s_ow"]], outs=[wt_te])
                fns = []
                for k in range(NDT):
                    mob, _ = mo[k]
                    for hh in range(NH):
                        fns.append(lambda E, k=k, hh=hh, wt=wt, mob=mob:
                                   E.matmul(self.ps[4 + hh][:, 0:HW_],
                                            wt[:, k * 128:(k + 1) * 128],
                                            mob[:, hh * HW_:(hh + 1) * HW_],
                                            start=(k == 0), stop=(k == NDT - 1)))
                self.mm_group(fns, ins=[wt_te] + moins,
                              outs=[self.pse[4 + hh2] for hh2 in range(NH)])
                og, og_te = ostg[fo % 2]
                for hh in range(NH):
                    self.op("act", lambda E, hh=hh, og=og: E.copy(
                        og[:, hh * HW_:(hh + 1) * HW_],
                        self.ps[4 + hh][:, 0:HW_]),
                        ins=[self.pse[4 + hh]], outs=[og_te])
                for s128 in range(TW2 // 128):
                    tt = (t0 + s128 * 128) // 128
                    rsl = self.r[:, tt * C + fo * 128: tt * C + (fo + 1) * 128]
                    self.transpose_addr(og[:, s128 * 128:(s128 + 1) * 128],
                                        og_te, rsl, self.rT[tt],
                                        pbank=6 + (fo % 2))
        self.apop()
        self.apop()
        self.barrier()


def host_constants(T):
    inv_freq = (1.0 / 10000.0 ** (np.arange(0, LAT, 2, dtype=np.float32) / LAT))
    t = np.arange(T, dtype=np.float32)
    freqs = np.outer(t, inv_freq.astype(np.float32)).astype(np.float32)
    cos = np.repeat(np.cos(freqs), 2, axis=-1)[:, :LAT].astype(np.float32)
    sin = np.repeat(np.sin(freqs), 2, axis=-1)[:, :LAT].astype(np.float32)
    cos4 = np.ascontiguousarray(np.tile(cos.T, (4, 1)))
    sin4 = np.ascontiguousarray(np.tile(sin.T, (4, 1)))
    R = np.zeros((LAT, LAT), np.float32)
    for i in range(0, LAT, 2):
        R[i, i + 1] = -1.0
        R[i + 1, i] = 1.0
    R4 = np.kron(np.eye(4, dtype=np.float32), R.T)
    ident = np.eye(128, dtype=np.float32)
    QCW = min(512, T)
    nm = min(4, max(1, QCW // 128))
    masks = np.zeros((nm * 128, QCW), np.float32)
    for mi in range(nm):
        off = mi * 128
        kl = np.arange(128)[:, None]
        ql = np.arange(QCW)[None, :]
        masks[mi * 128:(mi + 1) * 128] = (ql >= kl + off).astype(np.float32)
    return {"ident": ident, "identb": ident, "R4_t": R4, "cos4": cos4,
            "sin4": sin4, "ones_row": np.ones((1, 128), np.float32),
            "ones_m": np.ones((128, LAT), np.float32),
            "masks": masks}


def core_inputs(ins, T=2048):
    import ml_dtypes
    g = lambda k: np.ascontiguousarray(np.asarray(ins[k], dtype=np.float32))
    col = lambda k: np.ascontiguousarray(g(k).reshape(-1, 1))
    wqk_t = g("L0_attn_a_wqk_w").T
    m = {
        "c_attn_w": g("L0_attn_a_c_attn_w"),
        "c_attn_b": col("L0_attn_a_c_attn_b"),
        "q2l_w_t": np.ascontiguousarray(g("L0_attn_a_q2l_w").T),
        "kv2l_w_t": np.ascontiguousarray(g("L0_attn_a_kv2l_w").T),
        "q2l_b": col("L0_attn_a_q2l_b"),
        "kv2l_b": col("L0_attn_a_kv2l_b"),
        "kv2l_b_row": np.ascontiguousarray(
            g("L0_attn_a_kv2l_b").reshape(1, -1)),
        "l2o_w_t": np.ascontiguousarray(g("L0_attn_a_l2o_w").T),
        "l2o_b": col("L0_attn_a_l2o_b"),
        "wqk_w_t4": np.ascontiguousarray(
            np.kron(np.eye(4, dtype=np.float32), wqk_t)),
        "wqk_b4": np.ascontiguousarray(
            np.tile(g("L0_attn_a_wqk_b"), 4).reshape(-1, 1)),
        "cproj_w_t": np.ascontiguousarray(g("L0_attn_a_cproj_w").T),
        "cproj_b": col("L0_attn_a_cproj_b"),
        "ffn_wg": g("L0_ffn_w_gate"),
        "ffn_wu": g("L0_ffn_w_up"),
        "ffn_wd": g("L0_ffn_w_down"),
        "bb_gamma": col("L0_attn_b_gamma"),
        "bb_beta": col("L0_attn_b_beta"),
        "bb_in_w": g("L0_attn_b_in_w"),
        "bb_merge_w": g("L0_attn_b_merge_w"),
        "bb_out_w": g("L0_attn_b_out_w"),
    }
    for sfx in ("", "_b"):
        m["bb_conv_w" + sfx] = g("L0_attn_b_conv_w" + sfx)
        m["bb_conv_b" + sfx] = col("L0_attn_b_conv_b" + sfx)
        m["bb_xproj_w" + sfx] = g("L0_attn_b_xproj_w" + sfx)
        m["bb_dt_w" + sfx] = g("L0_attn_b_dt_w" + sfx)
        m["bb_dt_b" + sfx] = col("L0_attn_b_dt_b" + sfx)
        m["bb_A_log" + sfx] = g("L0_attn_b_A_log" + sfx)
        m["bb_D" + sfx] = col("L0_attn_b_D_skip" + sfx)
    cst = host_constants(T)
    cst["identb"] = cst["identb"].astype(ml_dtypes.bfloat16)
    cst["masks"] = cst["masks"].astype(ml_dtypes.bfloat16)
    m.update(cst)
    return m


PARTS = ("a", "f1", "b", "f2")


def kernel(**inputs):
    x = np.ascontiguousarray(np.asarray(inputs["L0_attn_a_x"], dtype=np.float32))
    B, T, _ = x.shape
    kb = KB(T=T, parts=PARTS)
    nc = kb.build()
    base = core_inputs(inputs, T)
    in_maps = []
    for i in range(B):
        mi = dict(base)
        mi["x"] = x[i]
        in_maps.append(mi)
    res = run_bass_kernel_spmd(nc, in_maps, core_ids=list(range(B)))
    out = np.stack([np.asarray(r["y"]) for r in res.results], axis=0)
    return out.astype(np.float32)


# version 36 (final)
# speedup vs baseline: 1.0093x; 1.0093x over previous
import contextlib
import numpy as np
import concourse.bass as bass
import concourse.mybir as mybir
from concourse.bass_utils import run_bass_kernel_spmd
from concourse.alu_op_type import AluOpType as OP



F32 = mybir.dt.float32
BF16 = mybir.dt.bfloat16
U8 = mybir.dt.uint8
AF = mybir.ActivationFunctionType

C = 1024
HID = 2816
DI = 2048
DST = 16
DTR = 64
NH = 16
HD = 64
LAT = 32
MIDS = (0.25, 0.75, 1.25, 1.75, 2.5, 3.5, 5.0)
DSIZE = {F32: 4, BF16: 2}


class TE:
    __slots__ = ("w", "rs")

    def __init__(self):
        self.w = None
        self.rs = []


class KB:
    ESEM = {"pe": "e_pe", "act": "e_act", "dve": "e_dve", "pool": "e_pool",
            "sync": "e_sync"}
    QSEM = {"sync": "q_sync", "pool": "q_pool", "act": "q_act"}
    ARENA_BYTES = 140 * 1024

    def __init__(self, T=2048, parts=("a", "f1", "b", "f2")):
        self.T = T
        self.parts = parts
        self.TT = T // 128
        self.QCW = min(512, T)
        self.NQC = T // self.QCW
        nc = self.nc = bass.Bass("TRN2", target_bir_lowering=False)
        self.es = contextlib.ExitStack()
        self.prog = []
        self.sems = {}
        self.cnt = {}
        self.hw = {e: {} for e in self.ESEM}
        for s in list(self.ESEM.values()):
            self.sems[s] = self.es.enter_context(nc.semaphore(s))
            self.cnt[s] = 0
        self.qpool = {}
        self.qidx = {}
        for qeng, npool in (("sync", 24), ("pool", 16), ("act", 8)):
            names = []
            for i in range(npool):
                s = f"dq_{qeng}_{i}"
                self.sems[s] = self.es.enter_context(nc.semaphore(s))
                self.cnt[s] = 0
                names.append(s)
            self.qpool[qeng] = names
            self.qidx[qeng] = 0
        self.sems["bgq"] = self.es.enter_context(nc.semaphore("bgq"))
        self.cnt["bgq"] = 0
        self.bg = None
        self.want_wq = (("a" in parts) and
                        ("f1" in parts or "f2" in parts))
        self.dram = {}
        self._declare_dram()
        self.r = self.es.enter_context(nc.sbuf_tensor("r", [128, self.TT * C], F32))
        self.rT = [TE() for _ in range(self.TT)]
        self.arena = self.es.enter_context(
            nc.sbuf_tensor("arena", [128, self.ARENA_BYTES], U8))
        self.aoff = 0
        self.amarks = []
        self.ps = [self.es.enter_context(nc.psum_tensor(f"p{i}", [128, 512],
                                                        F32)) for i in range(8)]
        self.pse = [TE() for _ in range(8)]
        self.named = {}

    def apush(self):
        self.amarks.append(self.aoff)

    def apop(self):
        self.aoff = self.amarks.pop()

    def aview(self, n, dtype=F32):
        nb = n * DSIZE[dtype]
        nb_al = (nb + 63) & ~63
        assert self.aoff + nb_al <= self.ARENA_BYTES, \
            f"arena overflow: {self.aoff + nb_al}"
        ap = self.arena[:, self.aoff:self.aoff + nb].bitcast(dtype)
        self.aoff += nb_al
        return ap, TE()

    def buf(self, name, shape=None, dtype=F32):
        if name not in self.named:
            h = self.es.enter_context(self.nc.sbuf_tensor(name, shape, dtype))
            self.named[name] = (h, TE())
        return self.named[name]

    def _wait(self, eng, ev):
        if ev is None:
            return
        s, v = ev
        if self.hw[eng].get(s, 0) >= v:
            return
        self.hw[eng][s] = v
        sem = self.sems[s]
        self.prog.append((eng, lambda E, sem=sem, v=v: E.wait_ge(sem, v)))

    def _dep_waits(self, eng, ins, outs, dma):
        my = self.ESEM[eng]
        for t in ins:
            self._wait(eng, t.w)
        for t in outs:
            if t.w is not None and (dma or t.w[0] != my):
                self._wait(eng, t.w)
            for rv in t.rs:
                self._wait(eng, rv)

    def _commit(self, eng, ins, outs, sname, amt):
        self.cnt[sname] += amt
        ev = (sname, self.cnt[sname])
        for t in ins:
            t.rs.append(ev)
        for t in outs:
            t.w = ev
            t.rs = []
        return ev

    def op(self, eng, fn, ins=(), outs=(), dma=False):
        self._dep_waits(eng, ins, outs, dma)
        if dma:
            pool = self.qpool[eng]
            s = pool[self.qidx[eng] % len(pool)]
            self.qidx[eng] += 1
            if self.cnt[s] > 0:
                self._wait(eng, (s, self.cnt[s]))
            amt = 16
        else:
            s = self.ESEM[eng]
            amt = 1
        ev = self._commit(eng, ins, outs, s, amt)
        sem = self.sems[s]
        self.prog.append((eng, lambda E, fn=fn, sem=sem, amt=amt:
                          fn(E).then_inc(sem, amt)))
        return ev

    def mm_group(self, fns, ins=(), outs=()):
        self._dep_waits("pe", ins, outs, False)
        for fn in fns[:-1]:
            self.prog.append(("pe", fn))
        ev = self._commit("pe", ins, outs, "e_pe", 1)
        sem = self.sems["e_pe"]
        fn = fns[-1]
        self.prog.append(("pe", lambda E, fn=fn, sem=sem: fn(E).then_inc(sem, 1)))
        return ev

    def dma(self, eng, out_ap, in_ap, ins=(), outs=(), slow=False):
        return self.op(eng, lambda E: E.dma_start(
            out=out_ap, in_=in_ap, allow_slow_non_contiguous=slow),
            ins=ins, outs=outs, dma=True)

    def bgdma(self, out_ap, in_ap, outs=()):
        ev = self._commit("pool", (), outs, "bgq", 16)
        sem = self.sems["bgq"]
        self.prog.append(("pool", lambda E, out_ap=out_ap, in_ap=in_ap,
                          sem=sem: E.dma_start(
                              out=out_ap, in_=in_ap).then_inc(sem, 16)))
        return ev

    def barrier(self):
        evs = [(s, v) for s, v in self.cnt.items() if v > 0 and s != "bgq"]
        for eng in self.ESEM:
            for ev in evs:
                self._wait(eng, ev)

    def bg_step(self, n=1):
        while n > 0 and self.bg is not None:
            try:
                next(self.bg)
            except StopIteration:
                self.bg = None
            n -= 1

    def din(self, name, shape, dtype=F32):
        self.dram[name] = self.nc.dram_tensor(name, shape, dtype,
                                              kind="ExternalInput").ap()
        return self.dram[name]

    def dscr(self, name, shape, dtype):
        self.dram[name] = self.nc.dram_tensor(name, shape, dtype,
                                              kind="Internal").ap()
        return self.dram[name]

    def _declare_dram(self):
        T = self.T
        self.din("x", [T, C])
        self.y_out = self.nc.dram_tensor("y", [T, C], F32,
                                         kind="ExternalOutput").ap()
        self.din("c_attn_w", [3 * C, C])
        self.din("c_attn_b", [3 * C, 1])
        self.din("q2l_w_t", [HD, LAT])
        self.din("kv2l_w_t", [HD, LAT])
        self.din("q2l_b", [LAT, 1])
        self.din("kv2l_b", [LAT, 1])
        self.din("l2o_w_t", [LAT, HD])
        self.din("l2o_b", [HD, 1])
        self.din("wqk_w_t4", [128, 128])
        self.din("wqk_b4", [128, 1])
        self.din("cproj_w_t", [C, C])
        self.din("cproj_b", [C, 1])
        self.din("ffn_wg", [HID, C])
        self.din("ffn_wu", [HID, C])
        self.din("ffn_wd", [C, HID])
        self.din("bb_gamma", [C, 1])
        self.din("bb_beta", [C, 1])
        self.din("bb_in_w", [C, 2 * DI])
        self.din("bb_merge_w", [2 * DI, DI])
        self.din("bb_out_w", [DI, C])
        for sfx in ("", "_b"):
            self.din("bb_conv_w" + sfx, [DI, 4])
            self.din("bb_conv_b" + sfx, [DI, 1])
            self.din("bb_xproj_w" + sfx, [DI, DTR + 2 * DST])
            self.din("bb_dt_w" + sfx, [DTR, DI])
            self.din("bb_dt_b" + sfx, [DI, 1])
            self.din("bb_A_log" + sfx, [DI, DST])
            self.din("bb_D" + sfx, [DI, 1])
        self.din("ident", [128, 128])
        self.din("identb", [128, 128], BF16)
        self.din("R4_t", [128, 128])
        self.din("cos4", [128, T])
        self.din("sin4", [128, T])
        self.din("ones_row", [1, 128])
        self.din("ones_m", [128, LAT])
        self.din("kv2l_b_row", [1, LAT])
        nm = min(4, max(1, self.QCW // 128))
        self.nmask = nm
        self.din("masks", [nm * 128, self.QCW], BF16)
        NIC = (HID + 511) // 512
        self.dscr("s_wg2", [128, NIC * 4096], BF16)
        self.dscr("s_wu2", [128, NIC * 4096], BF16)
        self.dscr("s_wd2", [128, 8 * HID], BF16)
        self.dscr("s_cpj", [C, C], F32)
        self.dscr("s_iw", [2 * DI, C], BF16)
        self.dscr("s_mw", [DI, 2 * DI], BF16)
        self.dscr("s_ow", [C, DI], BF16)
        self.dscr("s_xs_t", [DI, T], BF16)
        self.dscr("s_sz_t", [DI, T], BF16)
        self.dscr("s_yf_t", [DI, T], BF16)
        self.dscr("s_yb_t", [DI, T], BF16)

    def precast_b(self):
        self.scr_te = {"s_iw": TE(), "s_mw": TE(), "s_ow": TE()}
        for nm, srcnm, nstripe in (("s_iw", "bb_in_w", 2 * DI // 128),
                                   ("s_mw", "bb_merge_w", DI // 128),
                                   ("s_ow", "bb_out_w", C // 128)):
            for m in range(nstripe):
                self.dma(
                    "pool",
                    self.dram[nm][m * 128:(m + 1) * 128, :].rearrange(
                        "p (a f) -> p a f", f=128),
                    self.dram[srcnm][:, m * 128:(m + 1) * 128].rearrange(
                        "(a p) f -> p a f", p=128),
                    outs=[self.scr_te[nm]])

    def build(self):
        self.load_consts()
        x_r = self.dram["x"].rearrange("(a p) f -> p a f", p=128)
        self.dma("sync", self.r[:].rearrange("p (a f) -> p a f", f=C), x_r,
                 outs=self.rT)
        if "a" in self.parts:
            self.phase_attn_a()
            self.barrier()
        if "b" in self.parts:
            self.precast_b()
        if ("f1" in self.parts or "f2" in self.parts) and not self.want_wq:
            self.apush()
            self.phase_ffn_wquant()
            self.apop()
            self.barrier()
        if "f1" in self.parts:
            self.apush()
            self.phase_ffn()
            self.apop()
            self.barrier()
        if "b" in self.parts:
            self.phase_attn_b()
            self.barrier()
        if "f2" in self.parts:
            self.apush()
            self.phase_ffn()
            self.apop()
            self.barrier()
        y_r = self.y_out.rearrange("(a p) f -> p a f", p=128)
        ev = self.dma("sync", y_r, self.r[:].rearrange("p (a f) -> p a f", f=C),
                 ins=self.rT)
        self._wait("sync", ev)
        self.finish()
        return self.nc

    def finish(self):
        with self.es:
            with self.nc.Block() as block:
                @block.sync
                def _(E):
                    for eng, fn in self.prog:
                        if eng == "sync":
                            fn(E)

                @block.tensor
                def _(E):
                    for eng, fn in self.prog:
                        if eng == "pe":
                            fn(E)

                @block.scalar
                def _(E):
                    for eng, fn in self.prog:
                        if eng == "act":
                            fn(E)

                @block.vector
                def _(E):
                    for eng, fn in self.prog:
                        if eng == "dve":
                            fn(E)

                @block.gpsimd
                def _(E):
                    for eng, fn in self.prog:
                        if eng == "pool":
                            fn(E)

    def load_consts(self):
        self.cpj_te = TE()
        if "a" in self.parts:
            for fo in range(C // 128):
                self.dma("act",
                         self.dram["s_cpj"][fo * 128:(fo + 1) * 128, :].rearrange(
                             "p (a f) -> p a f", f=128),
                         self.dram["cproj_w_t"][:, fo * 128:(fo + 1) * 128].rearrange(
                             "(a p) f -> p a f", p=128),
                         outs=[self.cpj_te])
        idb, idb_te = self.buf("identb_sb", [128, 128], BF16)
        idf, idf_te = self.buf("identf_sb", [128, 128], F32)
        ones, ones_te = self.buf("ones_sb", [1, 128], BF16)
        self.dma("sync", idb[:], self.dram["identb"], outs=[idb_te])
        self.dma("sync", idf[:], self.dram["ident"], outs=[idf_te])
        self.dma("pool", ones[:], self.dram["ones_row"], outs=[ones_te])

    def layernorm(self, src_ap, src_te, out_ap, out_te, stash=None):
        st, st_te = self.buf("ln_stats", [128, 12], F32)
        mv, mv_te = self.buf("ln_mv", [128, 2], F32)
        rs, rs_te = self.buf("ln_rs", [128, 1], F32)
        half = C // 2
        self.op("dve", lambda E: E.bn_stats(st[:, 0:6], src_ap[:, 0:half]),
                ins=[src_te], outs=[st_te])
        self.op("dve", lambda E: E.bn_stats(st[:, 6:12], src_ap[:, half:C]),
                ins=[src_te], outs=[st_te])
        self.op("dve", lambda E: E.bn_aggr(mv[:, 0:2], st[:, 0:12]),
                ins=[st_te], outs=[mv_te])
        self.op("dve", lambda E: E.tensor_scalar(rs[:], mv[:, 1:2], 1e-5, None,
                                                 op0=OP.add),
                ins=[mv_te], outs=[rs_te])
        self.op("dve", lambda E: E.reciprocal(rs[:], rs[:]),
                ins=[rs_te], outs=[rs_te])
        self.op("act", lambda E: E.activation(rs[:], rs[:], AF.Sqrt),
                ins=[rs_te], outs=[rs_te])
        self.op("dve", lambda E: E.tensor_scalar(out_ap, src_ap, mv[:, 0:1],
                                                 rs[:, 0:1], op0=OP.subtract,
                                                 op1=OP.mult),
                ins=[src_te, mv_te, rs_te], outs=[out_te])
        if stash is not None:
            sa, sa_te = stash
            self.op("dve", lambda E: E.tensor_copy(sa[:, 0:1], mv[:, 0:1]),
                    ins=[mv_te], outs=[sa_te])
            self.op("dve", lambda E: E.tensor_copy(sa[:, 1:2], rs[:, 0:1]),
                    ins=[rs_te], outs=[sa_te])

    QCH = 1408

    def quant(self, x_ap, x_te, W, out_ap, out_te, qb):
        for _ in self.quant_gen(x_ap, x_te, W, out_ap, out_te, qb):
            pass

    def quant_gen(self, x_ap, x_te, W, out_ap, out_te, qb):
        for c0 in range(0, W, self.QCH):
            cw = min(self.QCH, W - c0)
            yield from self._quant_chunk(x_ap[:, c0:c0 + cw], x_te,
                                         out_ap[:, c0:c0 + cw], out_te, cw, qb)

    def _quant_chunk(self, x_ap, x_te, out_ap, out_te, W, qb):
        nb = W // 16
        (zt, zt_te), (ix, ix_te) = qb["t1"], qb["ix"]
        (bm, bm_te), (sc, sc_te), (rsc, rsc_te) = qb["bm"], qb["sc"], qb["rsc"]
        sch, sch_te = qb["sch"]
        xb = x_ap.rearrange("p (b e) -> p b e", e=16)
        self.op("dve", lambda E: E.tensor_reduce(
            bm[:, 0:nb], xb, axis=mybir.AxisListType.X, op=OP.max,
            apply_absolute_value=True), ins=[x_te], outs=[bm_te])
        yield
        C6A = float(np.float32(1.0 / 6.0))
        C6B = float(np.float32(1.0 / 6.0 - np.float32(1.0 / 6.0)))
        self.op("dve", lambda E: E.tensor_scalar(
            rsc[:, 0:nb], bm[:, 0:nb], C6A, None, op0=OP.mult),
            ins=[bm_te], outs=[rsc_te])
        self.op("dve", lambda E: E.scalar_tensor_tensor(
            sc[:, 0:nb], in0=bm[:, 0:nb], scalar=C6B, in1=rsc[:, 0:nb],
            op0=OP.mult, op1=OP.add), ins=[bm_te, rsc_te], outs=[sc_te])
        self.op("dve", lambda E: E.tensor_scalar(
            sc[:, 0:nb], sc[:, 0:nb], 1e-12, None, op0=OP.max),
            ins=[sc_te], outs=[sc_te])
        self.op("dve", lambda E: E.tensor_scalar(
            sch[:, 0:nb], sc[:, 0:nb], 0.5, None, op0=OP.mult),
            ins=[sc_te], outs=[sch_te])
        self.op("dve", lambda E: E.reciprocal(rsc[:, 0:nb], sc[:, 0:nb]),
                ins=[sc_te], outs=[rsc_te])
        yield
        rscb = rsc[:, 0:nb, None].broadcast_to([128, nb, 16])
        self.op("dve", lambda E: E.tensor_tensor(
            zt[:, 0:W].rearrange("p (b e) -> p b e", e=16), xb, rscb,
            op=OP.mult), ins=[x_te, rsc_te], outs=[zt_te])
        yield
        sg, sg_te = qb["sg"]
        self.op("dve", lambda E: E.tensor_scalar(
            sg[:, 0:W], x_ap, 0.0, None, op0=OP.is_gt),
            ins=[x_te], outs=[sg_te])
        yield
        self.op("dve", lambda E: E.tensor_scalar(
            sg[:, 0:W], sg[:, 0:W], 2.0, -1.0, op0=OP.mult, op1=OP.add),
            ins=[sg_te], outs=[sg_te])
        yield
        self.op("dve", lambda E: E.tensor_tensor(
            zt[:, 0:W], zt[:, 0:W], zt[:, 0:W], op=OP.mult),
            ins=[zt_te], outs=[zt_te])
        yield
        self.op("dve", lambda E: E.tensor_scalar(
            ix[:, 0:W], zt[:, 0:W], 0.0625, None, op0=OP.is_gt),
            ins=[zt_te], outs=[ix_te])
        yield
        for thr in (0.5625, 1.5625, 3.0625):
            self.op("dve", lambda E, thr=thr: E.scalar_tensor_tensor(
                ix[:, 0:W], in0=zt[:, 0:W], scalar=thr, in1=ix[:, 0:W],
                op0=OP.is_gt, op1=OP.add), ins=[zt_te, ix_te], outs=[ix_te])
            yield
        hi, hi_te = qb["hi"]
        self.op("dve", lambda E: E.tensor_scalar(
            hi[:, 0:W], zt[:, 0:W], 25.0, None, op0=OP.is_gt),
            ins=[zt_te], outs=[hi_te])
        yield
        for thr in (25.0, 12.25, 6.25):
            self.op("dve", lambda E, thr=thr: E.scalar_tensor_tensor(
                hi[:, 0:W], in0=zt[:, 0:W], scalar=thr, in1=hi[:, 0:W],
                op0=OP.is_gt, op1=OP.add), ins=[zt_te, hi_te], outs=[hi_te])
            yield
        self.op("dve", lambda E: E.scalar_tensor_tensor(
            ix[:, 0:W], in0=hi[:, 0:W], scalar=2.0, in1=ix[:, 0:W],
            op0=OP.mult, op1=OP.add), ins=[hi_te, ix_te], outs=[ix_te])
        yield
        self.op("dve", lambda E: E.tensor_tensor(
            sg[:, 0:W], ix[:, 0:W], sg[:, 0:W], op=OP.mult),
            ins=[ix_te, sg_te], outs=[sg_te])
        yield
        schb = sch[:, 0:nb, None].broadcast_to([128, nb, 16])
        self.op("dve", lambda E: E.tensor_tensor(
            out_ap.rearrange("p (b e) -> p b e", e=16),
            sg[:, 0:W].rearrange("p (b e) -> p b e", e=16), schb, op=OP.mult),
            ins=[sg_te, sch_te], outs=[out_te])
        yield

    def mk_qbufs(self, W):
        ch = min(W, self.QCH)
        cb = ch // 16
        return {"ix": self.aview(ch, BF16),
                "t1": self.aview(ch), "sg": self.aview(ch, BF16),
                "hi": self.aview(ch, BF16),
                "bm": self.aview(cb), "sc": self.aview(cb),
                "rsc": self.aview(cb), "sch": self.aview(cb)}

    def transpose128(self, src_ap, src_te, dst_ap, dst_te, pbank, bf=True,
                     add_to=False, width=128):
        idb, idb_te = self.buf("identb_sb", [128, 128], BF16)
        idf, idf_te = self.buf("identf_sb", [128, 128], F32)
        ident, ident_te = (idb, idb_te) if bf else (idf, idf_te)
        if bf:
            p = self.ps[pbank][:, 0:(width + 1) // 2].bitcast(BF16)
        else:
            p = self.ps[pbank][:, 0:width]
        pte = self.pse[pbank]
        self.op("pe", lambda E: E.transpose(p, src_ap, ident[:]),
                ins=[src_te, ident_te], outs=[pte])
        if not add_to:
            self.op("act", lambda E: E.copy(dst_ap, p),
                    ins=[pte], outs=[dst_te])
        else:
            self.op("dve", lambda E: E.tensor_tensor(dst_ap, dst_ap,
                                                     p, op=OP.add),
                    ins=[pte, dst_te], outs=[dst_te])

    def transpose_addr(self, src_ap, src_te, dst_ap, dst_te, pbank):
        idf, idf_te = self.buf("identf_sb", [128, 128], F32)
        p = self.ps[pbank][:, 0:128]
        pte = self.pse[pbank]
        fns = [lambda E: E.matmul(p, src_ap, idf[:], start=True, stop=False),
               lambda E: E.matmul(p, idf[:], dst_ap, start=False, stop=True)]
        self.mm_group(fns, ins=[src_te, dst_te, idf_te], outs=[pte])
        self.op("act", lambda E: E.copy(dst_ap, p), ins=[pte], outs=[dst_te])

    def phase_ffn_wquant(self):
        for _ in self.wquant_gen():
            pass

    def wquant_gen(self):
        av = self.aview
        win, win_te = av(1408, F32)
        wq, wq_te = av(1408, BF16)
        wstg, wstg_te = av(1408, BF16)
        qb = self.mk_qbufs(1408)
        yield
        for wname, scr in (("ffn_wg", "s_wg2"), ("ffn_wu", "s_wu2")):
            for i in range(HID // 128):
                self.dma("sync", win[:, 0:C],
                         self.dram[wname][i * 128:(i + 1) * 128, :],
                         ins=[win_te], outs=[win_te])
                yield
                yield from self.quant_gen(win[:, 0:C], win_te, C,
                                          wq[:, 0:C], wq_te, qb)
                ic, fo2 = i // 4, (i % 4) * 128
                for ft in range(C // 128):
                    self.transpose128(wq[:, ft * 128:(ft + 1) * 128], wq_te,
                                      wstg[:, ft * 128:(ft + 1) * 128],
                                      wstg_te, pbank=7)
                    yield
                for ft in range(C // 128):
                    o = ic * 4096 + ft * 512 + fo2
                    self.dma("sync", self.dram[scr][:, o:o + 128],
                             wstg[:, ft * 128:(ft + 1) * 128], ins=[wstg_te])
                yield
        for i in range(C // 128):
            for h0 in range(0, HID, 1408):
                hw = min(1408, HID - h0)
                self.dma("sync", win[:, 0:hw],
                         self.dram["ffn_wd"][i * 128:(i + 1) * 128, h0:h0 + hw],
                         ins=[win_te], outs=[win_te])
                yield
                yield from self.quant_gen(win[:, 0:hw], win_te, hw,
                                          wq[:, 0:hw], wq_te, qb)
                for jt in range(hw // 128):
                    self.transpose128(wq[:, jt * 128:(jt + 1) * 128], wq_te,
                                      wstg[:, jt * 128:(jt + 1) * 128],
                                      wstg_te, pbank=7)
                    yield
                self.dma("sync",
                         self.dram["s_wd2"][:, i * HID + h0:
                                            i * HID + h0 + hw],
                         wstg[:, 0:hw], ins=[wstg_te])
                yield

    def phase_ffn(self):
        TT = self.TT
        NIC = (HID + 511) // 512
        ics = [(i * 512, min(512, HID - i * 512)) for i in range(NIC)]
        qb = self.mk_qbufs(HID)
        ln, ln_te = self.aview(C, F32)
        xqp = [self.aview(C, BF16) for _ in range(2)]
        xqT = [[self.aview(C, BF16) for _ in range(2)] for _ in range(2)]
        h = [self.aview(HID, F32) for _ in range(2)]
        hsg = [self.aview(HID, BF16) for _ in range(2)]
        hqp = [self.aview(HID, BF16) for _ in range(2)]
        hqT = [self.aview(HID, BF16) for _ in range(2)]
        wgs = [self.aview(8 * 512, BF16) for _ in range(2)]
        wus = [self.aview(8 * 512, BF16) for _ in range(2)]
        wds = [self.aview(HID, BF16) for _ in range(2)]
        ybp = [self.aview(128, F32) for _ in range(2)]
        groups = [( [g0, g0 + 1] if g0 + 1 < TT else [g0] )
                  for g0 in range(0, TT, 2)]

        def s1q(gidx):
            for gi, tt in enumerate(groups[gidx]):
                rsl = self.r[:, tt * C:(tt + 1) * C]
                self.layernorm(rsl, self.rT[tt], ln[:, 0:C], ln_te)
                xq, xq_te = xqp[gi]
                self.quant(ln[:, 0:C], ln_te, C, xq[:, 0:C], xq_te, qb)

        def s1T(gidx):
            for gi, tt in enumerate(groups[gidx]):
                xq, xq_te = xqp[gi]
                xT, xT_te = xqT[gidx % 2][gi]
                for ft in range(C // 128):
                    self.transpose128(xq[:, ft * 128:(ft + 1) * 128], xq_te,
                                      xT[:, ft * 128:(ft + 1) * 128], xT_te,
                                      pbank=6 + (ft % 2))

        def s2_load(ici):
            ic0, icw = ics[ici]
            wg, wg_te = wgs[ici % 2]
            wu, wu_te = wus[ici % 2]
            if icw == 512:
                self.dma("sync", wg[:, 0:8 * 512],
                         self.dram["s_wg2"][:, ici * 4096:(ici + 1) * 4096],
                         outs=[wg_te])
                self.dma("sync", wu[:, 0:8 * 512],
                         self.dram["s_wu2"][:, ici * 4096:(ici + 1) * 4096],
                         outs=[wu_te])
            else:
                self.dma("sync",
                         wg[:, 0:8 * 512].rearrange(
                             "p (a b) -> p a b", b=512)[:, :, 0:icw],
                         self.dram["s_wg2"][:, ici * 4096:(ici + 1) * 4096]
                             .rearrange("p (a b) -> p a b", b=512)[:, :, 0:icw],
                         outs=[wg_te])
                self.dma("sync",
                         wu[:, 0:8 * 512].rearrange(
                             "p (a b) -> p a b", b=512)[:, :, 0:icw],
                         self.dram["s_wu2"][:, ici * 4096:(ici + 1) * 4096]
                             .rearrange("p (a b) -> p a b", b=512)[:, :, 0:icw],
                         outs=[wu_te])

        def s2(gidx):
            gts = groups[gidx]
            s2_load(0)
            for ici, (ic0, icw) in enumerate(ics):
                wg, wg_te = wgs[ici % 2]
                wu, wu_te = wus[ici % 2]
                if ici + 1 < len(ics):
                    s2_load(ici + 1)
                for gi, tt in enumerate(gts):
                    xT, xT_te = xqT[gidx % 2][gi]
                    pg, pg_te = self.ps[2 * gi], self.pse[2 * gi]
                    pu, pu_te = self.ps[2 * gi + 1], self.pse[2 * gi + 1]
                    fns = []
                    for k in range(C // 128):
                        st = (k == 0)
                        sp = (k == C // 128 - 1)
                        fns.append(lambda E, k=k, st=st, sp=sp, xT=xT, icw=icw,
                                   pg=pg, wg=wg:
                                   E.matmul(pg[:, 0:icw],
                                            xT[:, k * 128:(k + 1) * 128],
                                            wg[:, k * 512:k * 512 + icw],
                                            start=st, stop=sp))
                        fns.append(lambda E, k=k, st=st, sp=sp, xT=xT, icw=icw,
                                   pu=pu, wu=wu:
                                   E.matmul(pu[:, 0:icw],
                                            xT[:, k * 128:(k + 1) * 128],
                                            wu[:, k * 512:k * 512 + icw],
                                            start=st, stop=sp))
                    self.mm_group(fns, ins=[xT_te, wg_te, wu_te],
                                  outs=[pg_te, pu_te])
                    hs, hs_te = hsg[gi]
                    hb, hb_te = h[gi]
                    self.op("act", lambda E, ic0=ic0, icw=icw, pg=pg, hs=hs:
                            E.activation(hs[:, ic0:ic0 + icw], pg[:, 0:icw],
                                         AF.Silu),
                            ins=[pg_te], outs=[hs_te])
                    self.op("act", lambda E, ic0=ic0, icw=icw, pu=pu, hb=hb:
                            E.copy(hb[:, ic0:ic0 + icw], pu[:, 0:icw]),
                            ins=[pu_te], outs=[hb_te])

        def s2mul(gidx):
            for gi in range(len(groups[gidx])):
                hs, hs_te = hsg[gi]
                hb, hb_te = h[gi]
                self.op("dve", lambda E, hs=hs, hb=hb: E.tensor_tensor(
                    hb[:, 0:HID], hs[:, 0:HID], hb[:, 0:HID], op=OP.mult),
                    ins=[hs_te, hb_te], outs=[hb_te])

        def s3q(gidx):
            for gi in range(len(groups[gidx])):
                hb, hb_te = h[gi]
                hq, hq_te = hqp[gi]
                self.quant(hb[:, 0:HID], hb_te, HID, hq[:, 0:HID], hq_te, qb)

        def s3T(gidx):
            for gi in range(len(groups[gidx])):
                hq, hq_te = hqp[gi]
                hT, hT_te = hqT[gi]
                for jt in range(HID // 128):
                    self.transpose128(hq[:, jt * 128:(jt + 1) * 128], hq_te,
                                      hT[:, jt * 128:(jt + 1) * 128], hT_te,
                                      pbank=6 + (jt % 2))

        def s4(gidx):
            gts = groups[gidx]
            for fo in range(C // 128):
                wd, wd_te = wds[fo % 2]
                self.dma("sync", wd[:, 0:HID],
                         self.dram["s_wd2"][:, fo * HID:(fo + 1) * HID],
                         outs=[wd_te])
                for k in range(HID // 128):
                    for gi in range(len(gts)):
                        hT, hT_te = hqT[gi]
                        st = (k == 0)
                        sp = (k == HID // 128 - 1)
                        self.op("pe", lambda E, gi=gi, k=k, st=st, sp=sp,
                                wd=wd, hT=hT:
                                E.matmul(self.ps[4 + gi][:, 0:128],
                                         wd[:, k * 128:(k + 1) * 128],
                                         hT[:, k * 128:(k + 1) * 128],
                                         start=st, stop=sp),
                                ins=[wd_te, hT_te], outs=[self.pse[4 + gi]])
                for gi, tt in enumerate(gts):
                    yb, yb_te = ybp[gi % 2]
                    self.op("act", lambda E, gi=gi, yb=yb: E.copy(
                        yb[:, 0:128], self.ps[4 + gi][:, 0:128]),
                        ins=[self.pse[4 + gi]], outs=[yb_te])
                    rsl = self.r[:, tt * C + fo * 128: tt * C + (fo + 1) * 128]
                    self.transpose_addr(yb[:, 0:128], yb_te, rsl, self.rT[tt],
                                        pbank=6 + (gi % 2))

        ng = len(groups)
        s1q(0)
        s1T(0)
        if ng > 1:
            s1q(1)
            s1T(1)
        s2(0)
        s2mul(0)
        for g in range(ng):
            s3q(g)
            if g + 1 < ng:
                s2(g + 1)
                s2mul(g + 1)
            if g + 2 < ng:
                s1q(g + 2)
            s3T(g)
            s4(g)
            if g + 2 < ng:
                s1T(g + 2)

    def phase_attn_a(self):
        T, TT, QCW, NQC = self.T, self.TT, self.QCW, self.NQC
        KPC = QCW // 128
        NKT = T // 128
        av = self.aview
        self.apush()
        mask_sb = [av(QCW, BF16) for _ in range(self.nmask)]
        qfkT = [av(T, BF16) for _ in range(4)]
        khaT = [av(T, BF16) for _ in range(4)]
        vsb = [av(T, BF16) for _ in range(4)]
        cbias, cbias_te = av(16, F32)
        l2o4, l2o4_te = av(64, BF16)
        l2ob2, l2ob2_te = av(1, F32)
        onesm, onesm_te = av(LAT, BF16)
        cprojb, cprojb_te = av(8, F32)
        if self.want_wq:
            self.bg = self.wquant_gen()
            self.bg_step(1)
        self.apush()
        xnT, xnT_te = av(8 * T, BF16)
        cosb, cosb_te = av(T, BF16)
        sinb, sinb_te = av(T, BF16)
        R4b, R4b_te = av(128, BF16)
        wqk4b, wqk4b_te = av(128, BF16)
        mrg, mrg_te = av(3 * 8 * 128, BF16)
        cab, cab_te = av(48, BF16)
        latb = {}
        for nm2 in ("q2l_b", "kv2l_b"):
            latb[nm2] = av(1, F32)
        kvrow, kvrow_te = av(LAT, F32)
        pq2l, pq2l_te = av(LAT, BF16)
        pkv2l, pkv2l_te = av(LAT, BF16)
        mbq, mbq_te = av(1, F32)
        mbk, mbk_te = av(1, F32)
        vbr, vbr_te = av(128, BF16)
        sqb, sqb_te = av(T, BF16)
        crow, crow_te = av(16, BF16)
        m1, m1_te = av(16, F32)
        qlsp = [av(512, BF16) for _ in range(2)]
        qhat, qhat_te = av(512, BF16)
        tmp1, tmp1_te = av(512, BF16)
        tmp2, tmp2_te = av(512, BF16)
        cwb = [av(1024, BF16)]
        self.dma("pool", cosb, self.dram["cos4"], outs=[cosb_te])
        self.dma("pool", sinb, self.dram["sin4"], outs=[sinb_te])
        for mi in range(self.nmask):
            m, mte = mask_sb[mi]
            self.dma("sync", m, self.dram["masks"][mi * 128:(mi + 1) * 128, :],
                     outs=[mte])
        self.dma("pool", R4b, self.dram["R4_t"], outs=[R4b_te])
        self.dma("pool", wqk4b, self.dram["wqk_w_t4"], outs=[wqk4b_te])
        self.dma("pool", cab[0:64, 0:48],
                 self.dram["c_attn_b"].rearrange("(a p) o -> p (a o)", p=64),
                 outs=[cab_te], slow=True)
        for blk in range(4):
            self.dma("sync", latb["q2l_b"][0][32 * blk:32 * blk + 32, 0:1],
                     self.dram["q2l_b"], outs=[latb["q2l_b"][1]])
            self.dma("sync", latb["kv2l_b"][0][32 * blk:32 * blk + 32, 0:1],
                     self.dram["kv2l_b"], outs=[latb["kv2l_b"][1]])
            self.dma("pool", l2o4[32 * blk:32 * blk + 32, 0:64],
                     self.dram["l2o_w_t"], outs=[l2o4_te])
        self.dma("pool", pq2l[0:64, 0:LAT], self.dram["q2l_w_t"], outs=[pq2l_te])
        self.dma("pool", pkv2l[0:64, 0:LAT], self.dram["kv2l_w_t"], outs=[pkv2l_te])
        self.dma("sync", kvrow[0:1, 0:LAT], self.dram["kv2l_b_row"], outs=[kvrow_te])
        for half in range(2):
            self.dma("sync", l2ob2[64 * half:64 * half + 64, 0:1],
                     self.dram["l2o_b"], outs=[l2ob2_te])
        self.dma("pool", onesm, self.dram["ones_m"], outs=[onesm_te])
        ones_sb, ones_te = self.buf("ones_sb", [1, 128], BF16)
        self.dma("sync", cprojb[0:128, 0:8],
                 self.dram["cproj_b"].rearrange("(a p) o -> p (a o)", p=128),
                 outs=[cprojb_te], slow=True)
        ln, ln_te = av(C, BF16)
        for tt in range(TT):
            rsl = self.r[:, tt * C:(tt + 1) * C]
            self.layernorm(rsl, self.rT[tt], ln[:, 0:C], ln_te)
            for ft in range(8):
                self.transpose128(ln[:, ft * 128:(ft + 1) * 128], ln_te,
                                  xnT[:, ft * T + tt * 128: ft * T + tt * 128 + 128],
                                  xnT_te, pbank=6 + (ft % 2))
        PR = {"q": 0, "k": 1, "v": 2}
        for g in range(4):
            self.op("dve", lambda E: E.memset(vbr[0:1, 0:128], 1.0),
                    outs=[vbr_te])
            for proj, pi in PR.items():
                pw, pw_te = (pq2l, pq2l_te) if proj == "q" else (pkv2l, pkv2l_te)
                wrow = {"q": 0, "k": 1, "v": 2}[proj]
                for hh in range(4):
                    self.bg_step(4)
                    h = 4 * g + hh
                    cw, cw_te = cwb[0]
                    self.dma("pool", cw[0:64, 0:C],
                             self.dram["c_attn_w"][wrow * C + h * HD: wrow * C + h * HD + HD, :],
                             outs=[cw_te])
                    fns = []
                    for ft in range(8):
                        fns.append(lambda E, ft=ft, cw=cw, pw=pw:
                                   E.matmul(self.ps[0][:, ft * 32:(ft + 1) * 32],
                                            cw[0:64, ft * 128:(ft + 1) * 128],
                                            pw[0:64, 0:LAT], start=True, stop=True))
                    self.mm_group(fns, ins=[cw_te, pw_te], outs=[self.pse[0]])
                    for ft in range(8):
                        self.op("act", lambda E, ft=ft, pi=pi, hh=hh:
                                E.copy(mrg[:, (pi * 8 + ft) * 128 + hh * 32:
                                           (pi * 8 + ft) * 128 + hh * 32 + 32],
                                       self.ps[0][:, ft * 32:(ft + 1) * 32]),
                                ins=[self.pse[0]], outs=[mrg_te])
                    bq_ap = cab[0:64, wrow * 16 + h: wrow * 16 + h + 1]
                    if proj in ("q", "k"):
                        self.op("pe", lambda E, hh=hh, pw=pw, bq_ap=bq_ap:
                                E.matmul(self.ps[1][32 * hh:32 * hh + 32, 0:1],
                                         pw[0:64, 0:LAT], bq_ap, start=True,
                                         stop=True, tile_position=(0, 32 * hh)),
                                ins=[pw_te, cab_te], outs=[self.pse[1]])
                        mb, mb_te = (mbq, mbq_te) if proj == "q" else (mbk, mbk_te)
                        lb, lb_te = latb["q2l_b" if proj == "q" else "kv2l_b"]
                        self.op("dve", lambda E, hh=hh, mb=mb, lb=lb:
                                E.tensor_tensor(mb[32 * hh:32 * hh + 32, 0:1],
                                                self.ps[1][32 * hh:32 * hh + 32, 0:1],
                                                lb[32 * hh:32 * hh + 32, 0:1],
                                                op=OP.add),
                                ins=[self.pse[1], lb_te], outs=[mb_te])
                    else:
                        self.op("pe", lambda E, bq_ap=bq_ap, pw=pw:
                                E.matmul(self.ps[1][0:1, 0:LAT], bq_ap,
                                         pw[0:64, 0:LAT], start=True, stop=True),
                                ins=[pw_te, cab_te], outs=[self.pse[1]])
                        self.op("dve", lambda E, hh=hh:
                                E.tensor_tensor(vbr[0:1, hh * 32:hh * 32 + 32],
                                                self.ps[1][0:1, 0:LAT],
                                                kvrow[0:1, 0:LAT], op=OP.add),
                                ins=[self.pse[1], kvrow_te], outs=[vbr_te])
            qfk, qfk_te = qfkT[g]
            kha, kha_te = khaT[g]
            vg, vg_te = vsb[g]
            for tc in range(NQC):
                t0 = tc * QCW
                qls, qls_te = qlsp[tc % 2]
                for proj in ("q", "k"):
                    self.bg_step(4)
                    pi = PR[proj]
                    fns = []
                    for hh in range(4):
                        for k in range(8):
                            fns.append(lambda E, k=k, hh=hh, pi=pi, t0=t0:
                                       E.matmul(self.ps[0][32 * hh:32 * hh + 32, 0:QCW],
                                                mrg[:, (pi * 8 + k) * 128 + hh * 32:
                                                    (pi * 8 + k) * 128 + hh * 32 + 32],
                                                xnT[:, k * T + t0: k * T + t0 + QCW],
                                                start=(k == 0), stop=(k == 7),
                                                tile_position=(0, 32 * hh)))
                    self.mm_group(fns, ins=[mrg_te, xnT_te], outs=[self.pse[0]])
                    mb, mb_te = (mbq, mbq_te) if proj == "q" else (mbk, mbk_te)
                    self.op("act", lambda E, mb=mb, qls=qls: E.activation(
                        qls[:, 0:QCW], self.ps[0][:, 0:QCW], AF.Identity,
                        bias=mb[:, 0:1]), ins=[self.pse[0], mb_te], outs=[qls_te])
                    self.op("pe", lambda E, qls=qls: E.matmul(
                        self.ps[1][:, 0:QCW], R4b, qls[:, 0:QCW],
                        start=True, stop=True), ins=[R4b_te, qls_te],
                        outs=[self.pse[1]])
                    self.op("dve", lambda E, t0=t0: E.tensor_tensor(
                        tmp1[:, 0:QCW], self.ps[1][:, 0:QCW],
                        sinb[:, t0:t0 + QCW], op=OP.mult),
                        ins=[self.pse[1], sinb_te], outs=[tmp1_te])
                    self.op("dve", lambda E, t0=t0, qls=qls: E.tensor_tensor(
                        tmp2[:, 0:QCW], qls[:, 0:QCW], cosb[:, t0:t0 + QCW],
                        op=OP.mult), ins=[qls_te, cosb_te], outs=[tmp2_te])
                    if proj == "k":
                        self.op("dve", lambda E, t0=t0, kha=kha: E.tensor_tensor(
                            kha[:, t0:t0 + QCW], tmp1[:, 0:QCW], tmp2[:, 0:QCW],
                            op=OP.add), ins=[tmp1_te, tmp2_te], outs=[kha_te])
                    else:
                        self.op("dve", lambda E: E.tensor_tensor(
                            qhat[:, 0:QCW], tmp1[:, 0:QCW], tmp2[:, 0:QCW],
                            op=OP.add), ins=[tmp1_te, tmp2_te], outs=[qhat_te])
                        self.op("pe", lambda E: E.matmul(
                            self.ps[2][:, 0:QCW], wqk4b, qhat[:, 0:QCW],
                            start=True, stop=True), ins=[wqk4b_te, qhat_te],
                            outs=[self.pse[2]])
                        wb, wb_te = self.buf("wqkb4_sb", [128, 1], F32)
                        if wb_te.w is None:
                            self.dma("sync", wb[:], self.dram["wqk_b4"],
                                     outs=[wb_te])
                        self.op("act", lambda E, t0=t0, qfk=qfk: E.activation(
                            qfk[:, t0:t0 + QCW], self.ps[2][:, 0:QCW],
                            AF.Identity, bias=wb[:, 0:1]),
                            ins=[self.pse[2], wb_te], outs=[qfk_te])
                for s128 in range(QCW // 128):
                    self.bg_step(2)
                    tt128 = (t0 + s128 * 128) // 128
                    fns = []
                    for k in range(8):
                        fns.append(lambda E, k=k, tt128=tt128:
                                   E.matmul(self.ps[3][:, 0:128],
                                            xnT[:, k * T + tt128 * 128:
                                                k * T + tt128 * 128 + 128],
                                            mrg[:, (2 * 8 + k) * 128:
                                                (2 * 8 + k) * 128 + 128],
                                            start=(k == 0), stop=False))
                    fns.append(lambda E: E.matmul(
                        self.ps[3][:, 0:128], ones_sb[0:1, 0:128],
                        vbr[0:1, 0:128], start=False, stop=True))
                    self.mm_group(fns, ins=[mrg_te, xnT_te, vbr_te, ones_te],
                                  outs=[self.pse[3]])
                    self.op("act", lambda E, tt128=tt128, vg=vg: E.copy(
                        vg[:, tt128 * 128:(tt128 + 1) * 128],
                        self.ps[3][:, 0:128]), ins=[self.pse[3]], outs=[vg_te])
            for src_, src_te_, which in ((qfk, qfk_te, 0), (kha, kha_te, 1)):
                self.op("dve", lambda E, src_=src_: E.tensor_tensor(
                    sqb[:, 0:T], src_[:, 0:T], src_[:, 0:T], op=OP.mult),
                    ins=[src_te_], outs=[sqb_te])
                nch = (T + 511) // 512
                for hh in range(4):
                    self.bg_step(2)
                    h = 4 * g + hh
                    for ch in range(nch):
                        c0 = ch * 512
                        cw_ = min(512, T - c0)
                        self.op("pe", lambda E, hh=hh, c0=c0, cw_=cw_:
                                E.matmul(self.ps[hh][0:1, 0:cw_],
                                         onesm[32 * hh:32 * hh + 32, 0:1],
                                         sqb[32 * hh:32 * hh + 32, c0:c0 + cw_],
                                         start=True, stop=True,
                                         tile_position=(32 * hh, 0)),
                                ins=[sqb_te, onesm_te], outs=[self.pse[hh]])
                        self.op("dve", lambda E, hh=hh, ch=ch, cw_=cw_:
                                E.tensor_reduce(m1[0:1, ch:ch + 1],
                                                self.ps[hh][0:1, 0:cw_],
                                                axis=mybir.AxisListType.X,
                                                op=OP.max),
                                ins=[self.pse[hh]], outs=[m1_te])
                    self.op("dve", lambda E, nch=nch: E.tensor_reduce(
                        m1[0:1, 8:9], m1[0:1, 0:nch], axis=mybir.AxisListType.X,
                        op=OP.max), ins=[m1_te], outs=[m1_te])
                    self.op("act", lambda E: E.activation(
                        m1[0:1, 8:9], m1[0:1, 8:9], AF.Sqrt),
                        ins=[m1_te], outs=[m1_te])
                    if which == 0:
                        self.op("dve", lambda E, h=h: E.tensor_copy(
                            crow[0:1, h:h + 1], m1[0:1, 8:9]),
                            ins=[m1_te], outs=[crow_te])
                    else:
                        self.op("dve", lambda E, h=h: E.scalar_tensor_tensor(
                            crow[0:1, h:h + 1], in0=m1[0:1, 8:9],
                            scalar=-float(LAT) ** -0.5, in1=crow[0:1, h:h + 1],
                            op0=OP.mult, op1=OP.mult),
                            ins=[m1_te, crow_te], outs=[crow_te])
        self.op("pe", lambda E: E.matmul(self.ps[0][:, 0:16], ones_sb[0:1, 0:128],
                                         crow[0:1, 0:16], start=True, stop=True),
                ins=[crow_te, ones_te], outs=[self.pse[0]])
        self.op("act", lambda E: E.copy(cbias[:, 0:16], self.ps[0][:, 0:16]),
                ins=[self.pse[0]], outs=[cbias_te])
        self.apop()
        self.barrier()
        self.apush()
        attb = [av(QCW, BF16) for _ in range(8)]
        riv = [av(512, F32) for _ in range(2)]
        ylat = [av(512, BF16) for _ in range(2)]
        yT, yT_te = av(8 * 512, BF16)
        Lbp = [av(512, BF16) for _ in range(2)]
        cpw = [av(1024, BF16) for _ in range(2)]
        SC = float(LAT) ** -0.5
        for qc in range(NQC):
            q0 = qc * QCW
            ktmax = qc * KPC + KPC - 1
            for g in range(4):
                qfk, qfk_te = qfkT[g]
                kha, kha_te = khaT[g]
                vg, vg_te = vsb[g]
                rv, rv_te = riv[g % 2]
                yl, yl_te = ylat[g % 2]
                bav = 4 + (g % 2)
                bsm = 6
                def emit_qk(kt):
                    for j in range(4):
                        self.op("pe", lambda E, j=j, kt=kt, q0=q0, kha=kha,
                                qfk=qfk:
                                E.matmul(self.ps[j][:, 0:QCW],
                                         kha[32 * j:32 * j + 32,
                                             kt * 128:kt * 128 + 128],
                                         qfk[32 * j:32 * j + 32, q0:q0 + QCW],
                                         start=True, stop=True,
                                         tile_position=(32 * j, 0)),
                                ins=[kha_te, qfk_te], outs=[self.pse[j]])

                emit_qk(0)
                for kt in range(ktmax + 1):
                    self.bg_step(7)
                    mi = kt - qc * KPC
                    par = 4 * (kt % 2)
                    st_ = (kt == 0)
                    sp_ = (kt == ktmax)
                    for j in range(4):
                        ab, ab_te = attb[par + j]
                        h = 4 * g + j
                        self.op("act", lambda E, j=j, h=h, ab=ab: E.activation(
                            ab[:, 0:QCW], self.ps[j][:, 0:QCW], AF.Exp,
                            scale=SC, bias=cbias[:, h:h + 1]),
                            ins=[self.pse[j], cbias_te], outs=[ab_te])
                        if mi >= 0:
                            mk, mk_te = mask_sb[mi]
                            self.op("dve", lambda E, ab=ab, mk=mk:
                                    E.tensor_tensor(ab[:, 0:QCW], ab[:, 0:QCW],
                                                    mk[:, 0:QCW], op=OP.mult),
                                    ins=[mk_te, ab_te], outs=[ab_te])
                    if kt < ktmax:
                        emit_qk(kt + 1)
                    fns = []
                    for j in range(4):
                        ab, ab_te = attb[par + j]
                        fns.append(lambda E, j=j, kt=kt, ab=ab, vg=vg, bav=bav,
                                   st_=st_, sp_=sp_:
                                   E.matmul(self.ps[bav][32 * j:32 * j + 32, 0:QCW],
                                            vg[:, kt * 128 + 32 * j:
                                               kt * 128 + 32 * j + 32],
                                            ab[:, 0:QCW], start=st_, stop=sp_,
                                            tile_position=(0, 32 * j),
                                            skip_group_check=True))
                    self.mm_group(fns, ins=[attb[par][1], attb[par + 1][1],
                                            attb[par + 2][1], attb[par + 3][1],
                                            vg_te],
                                  outs=[self.pse[bav]])
                    fns = []
                    for j in range(4):
                        ab, ab_te = attb[par + j]
                        fns.append(lambda E, j=j, ab=ab, bsm=bsm,
                                   st_=st_, sp_=sp_:
                                   E.matmul(self.ps[bsm][32 * j:32 * j + 32, 0:QCW],
                                            onesm[:, 0:LAT], ab[:, 0:QCW],
                                            start=st_, stop=sp_,
                                            tile_position=(0, 32 * j),
                                            skip_group_check=True))
                    self.mm_group(fns, ins=[attb[par][1], attb[par + 1][1],
                                            attb[par + 2][1], attb[par + 3][1],
                                            onesm_te],
                                  outs=[self.pse[bsm]])
                self.bg_step(3)
                self.op("dve", lambda E, bsm=bsm, rv=rv: E.reciprocal(
                    rv[:, 0:QCW], self.ps[bsm][:, 0:QCW]),
                    ins=[self.pse[bsm]], outs=[rv_te])
                self.op("dve", lambda E, bav=bav, rv=rv, yl=yl: E.tensor_tensor(
                    yl[:, 0:QCW], self.ps[bav][:, 0:QCW], rv[:, 0:QCW],
                    op=OP.mult), ins=[self.pse[bav], rv_te], outs=[yl_te])
                for j in range(4):
                    h = 4 * g + j
                    bank = j // 2
                    hoff = 64 * (j % 2)
                    self.op("pe", lambda E, j=j, bank=bank, hoff=hoff, yl=yl:
                            E.matmul(self.ps[bank][hoff:hoff + 64, 0:QCW],
                                     l2o4[32 * j:32 * j + 32, 0:64],
                                     yl[32 * j:32 * j + 32, 0:QCW],
                                     start=True, stop=True,
                                     tile_position=(32 * j, hoff)),
                            ins=[yl_te, l2o4_te], outs=[self.pse[bank]])
                for bank in range(2):
                    pt = 2 * g + bank
                    self.op("act", lambda E, bank=bank, pt=pt: E.activation(
                        yT[:, pt * 512: pt * 512 + QCW], self.ps[bank][:, 0:QCW],
                        AF.Identity, bias=l2ob2[:, 0:1]),
                        ins=[self.pse[bank], l2ob2_te], outs=[yT_te])
            for fo in range(8):
                self.bg_step(2)
                bank = fo % 2
                cp, cp_te = cpw[fo % 2]
                Lb, Lb_te = Lbp[fo % 2]
                self.dma("pool", cp[:, 0:1024],
                         self.dram["s_cpj"][fo * 128:(fo + 1) * 128, :],
                         ins=[self.cpj_te, cp_te], outs=[cp_te])
                for fi in range(8):
                    self.op("pe", lambda E, bank=bank, fi=fi, cp=cp:
                            E.matmul(self.ps[bank][:, 0:QCW],
                                     cp[:, fi * 128:(fi + 1) * 128],
                                     yT[:, fi * 512: fi * 512 + QCW],
                                     start=(fi == 0), stop=(fi == 7)),
                            ins=[cp_te, yT_te], outs=[self.pse[bank]])
                self.op("act", lambda E, bank=bank, fo=fo, Lb=Lb: E.activation(
                    Lb[:, 0:QCW], self.ps[bank][:, 0:QCW], AF.Identity,
                    bias=cprojb[:, fo:fo + 1]),
                    ins=[self.pse[bank], cprojb_te], outs=[Lb_te])
                for s128 in range(QCW // 128):
                    tt = (q0 + s128 * 128) // 128
                    rsl = self.r[:, tt * C + fo * 128: tt * C + (fo + 1) * 128]
                    self.transpose128(Lb[:, s128 * 128:(s128 + 1) * 128], Lb_te,
                                      rsl, self.rT[tt], pbank=7,
                                      add_to=True)
        while self.bg is not None:
            self.bg_step(64)
        self.apop()
        self.apop()

    def phase_attn_b(self):
        T, TT = self.T, self.TT
        SEGW = min(512, T)
        NSEG = T // SEGW
        NDT = DI // 128
        av = self.aview
        self.apush()
        msb, msb_te = av(2 * TT, F32)
        self.apush()
        ln1, ln1_te = av(C, F32)
        ln2, ln2_te = av(C, BF16)
        ln2T, ln2T_te = av(8 * T, BF16)
        gcol, gcol_te = av(8, F32)
        bcol, bcol_te = av(8, BF16)
        bprime, bprime_te = av(32, F32)
        wiw = [av(C, BF16) for _ in range(2)]
        stgf = [av(512, BF16) for _ in range(2)]
        stgb = [av(512, BF16) for _ in range(2)]
        self.dma("sync", gcol[:, 0:8],
                 self.dram["bb_gamma"].rearrange("(a p) o -> p (a o)", p=128),
                 outs=[gcol_te], slow=True)
        self.dma("pool", bcol[:, 0:8],
                 self.dram["bb_beta"].rearrange("(a p) o -> p (a o)", p=128),
                 outs=[bcol_te], slow=True)
        for tt in range(TT):
            rsl = self.r[:, tt * C:(tt + 1) * C]
            self.layernorm(rsl, self.rT[tt], ln1[:, 0:C], ln1_te,
                           stash=(msb[:, 2 * tt:2 * tt + 2], msb_te))
            self.layernorm(ln1[:, 0:C], ln1_te, ln2[:, 0:C], ln2_te)
            for ft in range(8):
                p = self.ps[6 + (ft % 2)][:, 0:64].bitcast(BF16)
                pte = self.pse[6 + (ft % 2)]
                idb, idb_te = self.buf("identb_sb", [128, 128], BF16)
                self.op("pe", lambda E, ft=ft, p=p: E.transpose(
                    p, ln2[:, ft * 128:(ft + 1) * 128], idb[:]),
                    ins=[ln2_te, idb_te], outs=[pte])
                self.op("act", lambda E, ft=ft, tt=tt, p=p: E.activation(
                    ln2T[:, ft * T + tt * 128: ft * T + tt * 128 + 128],
                    p, AF.Identity, scale=gcol[:, ft:ft + 1]),
                    ins=[pte, gcol_te], outs=[ln2T_te])
            self.op("dve", lambda E, rsl=rsl: E.tensor_tensor(
                rsl, rsl, ln1[:, 0:C], op=OP.add),
                ins=[ln1_te], outs=[self.rT[tt]])
        tw = min(512, T)
        for m in range(32):
            wt, wt_te = wiw[m % 2]
            self.dma("sync", wt[:, 0:C],
                     self.dram["s_iw"][m * 128:(m + 1) * 128, :],
                     ins=[wt_te, self.scr_te["s_iw"]], outs=[wt_te])
            fns = []
            for k in range(8):
                fns.append(lambda E, k=k, m=m, wt=wt: E.matmul(
                    self.ps[2][:, m:m + 1], wt[:, k * 128:(k + 1) * 128],
                    bcol[:, k:k + 1], start=(k == 0), stop=(k == 7)))
            self.mm_group(fns, ins=[wt_te, bcol_te], outs=[self.pse[2]])
            self.op("act", lambda E, m=m: E.copy(
                bprime[:, m:m + 1], self.ps[2][:, m:m + 1]),
                ins=[self.pse[2]], outs=[bprime_te])
            for tc in range(T // tw):
                t0 = tc * tw
                bank = tc % 2
                fns = []
                for k in range(8):
                    fns.append(lambda E, bank=bank, k=k, t0=t0, tw=tw, wt=wt:
                               E.matmul(self.ps[bank][:, 0:tw],
                                        wt[:, k * 128:(k + 1) * 128],
                                        ln2T[:, k * T + t0: k * T + t0 + tw],
                                        start=(k == 0), stop=(k == 7)))
                self.mm_group(fns, ins=[wt_te, ln2T_te], outs=[self.pse[bank]])
                if m < 16:
                    sg_, sg_te_ = stgf[tc % 2]
                    self.op("act", lambda E, bank=bank, m=m, tw=tw, sg_=sg_:
                            E.activation(sg_[:, 0:tw], self.ps[bank][:, 0:tw],
                                         AF.Identity, bias=bprime[:, m:m + 1]),
                            ins=[self.pse[bank], bprime_te], outs=[sg_te_])
                    self.dma("sync", self.dram["s_xs_t"][m * 128:(m + 1) * 128,
                                                         t0:t0 + tw],
                             sg_[:, 0:tw], ins=[sg_te_])
                else:
                    sg_, sg_te_ = stgb[tc % 2]
                    self.op("act", lambda E, bank=bank, m=m, tw=tw, sg_=sg_:
                            E.activation(sg_[:, 0:tw], self.ps[bank][:, 0:tw],
                                         AF.Silu, bias=bprime[:, m:m + 1]),
                            ins=[self.pse[bank], bprime_te], outs=[sg_te_])
                    self.dma("sync", self.dram["s_sz_t"][(m - 16) * 128:
                                                         (m - 15) * 128,
                                                         t0:t0 + tw],
                             sg_[:, 0:tw], ins=[sg_te_])
        self.apop()
        self.barrier()
        for rev, sfx, yscr in ((True, "_b", "s_yb_t"), (False, "", "s_yf_t")):
            self.apush()
            asb, asb_te = av(16 * DST, F32)
            dtbc, dtbc_te = av(16, F32)
            cwc, cwc_te = av(16 * 4, F32)
            cbc, cbc_te = av(16, F32)
            dskc, dskc_te = av(16, F32)
            xpw, xpw_te = av(16 * 96, BF16)
            dtw, dtw_te = av(DI, BF16)
            astg, astg_te = av(DST, F32)
            self.dma("sync", dtbc[:, 0:16],
                     self.dram["bb_dt_b" + sfx].rearrange("(a p) o -> p (a o)", p=128),
                     outs=[dtbc_te], slow=True)
            self.dma("sync", cwc[:, 0:64].rearrange("p (a k) -> p a k", k=4),
                     self.dram["bb_conv_w" + sfx].rearrange("(a p) k -> p a k", p=128),
                     outs=[cwc_te], slow=True)
            self.dma("sync", cbc[:, 0:16],
                     self.dram["bb_conv_b" + sfx].rearrange("(a p) o -> p (a o)", p=128),
                     outs=[cbc_te], slow=True)
            self.dma("sync", dskc[:, 0:16],
                     self.dram["bb_D" + sfx].rearrange("(a p) o -> p (a o)", p=128),
                     outs=[dskc_te], slow=True)
            self.dma("pool", dtw[0:64, 0:DI], self.dram["bb_dt_w" + sfx],
                     outs=[dtw_te])
            for dt in range(NDT):
                self.dma("pool", xpw[:, dt * 96:(dt + 1) * 96],
                         self.dram["bb_xproj_w" + sfx][dt * 128:(dt + 1) * 128, :],
                         outs=[xpw_te])
                self.dma("sync", astg[:, 0:DST],
                         self.dram["bb_A_log" + sfx][dt * 128:(dt + 1) * 128, :],
                         ins=[astg_te], outs=[astg_te])
                self.op("act", lambda E: E.activation(astg[:, 0:DST],
                                                      astg[:, 0:DST], AF.Exp),
                        ins=[astg_te], outs=[astg_te])
                self.op("dve", lambda E, dt=dt: E.tensor_scalar(
                    asb[:, dt * DST:(dt + 1) * DST], astg[:, 0:DST], -1.0, None,
                    op0=OP.mult), ins=[astg_te], outs=[asb_te])
            dgb, dgb_te = av(64 * 128, BF16)
            idb_, idb_te_ = self.buf("identb_sb", [128, 128], BF16)
            for dt in range(NDT):
                for k in range(4):
                    self.op("dve", lambda E, dt=dt, k=k: E.tensor_scalar(
                        dgb[:, (dt * 4 + k) * 128:(dt * 4 + k + 1) * 128],
                        idb_[:], cwc[:, dt * 4 + k:dt * 4 + k + 1], None,
                        op0=OP.mult), ins=[idb_te_, cwc_te], outs=[dgb_te])
            xcb, xcb_te = av(16 * SEGW, BF16)
            brep, brep_te = av(16 * SEGW, BF16)
            crep, crep_te = av(16 * SEGW, BF16)
            dtin, dtin_te = av(SEGW, BF16)
            bcf, bcf_te = av(SEGW, BF16)

            dttp = [av(SEGW, BF16) for _ in range(2)]
            dtt_tes = [TE() for _ in range(NDT)]
            xss = [av(SEGW + 16, BF16) for _ in range(3)]
            spxp = [av(SEGW, F32) for _ in range(2)]
            sptp = [av(SEGW, F32) for _ in range(2)]
            slabA, _ = av(16 * SEGW, BF16)
            slabA_tes = [TE() for _ in range(DST)]
            bflat = slabA
            slabBp = []
            for _p in range(2):
                ap_, _ = av(16 * SEGW, BF16)
                slabBp.append((ap_, [TE() for _ in range(DST)]))
            cflat = slabBp[1][0]
            du, du_te = av(SEGW, BF16)
            carry, carry_te = av(16 * DST, F32)
            ystgp = [av(SEGW, BF16) for _ in range(2)]
            segs = list(range(NSEG))
            if rev:
                segs = segs[::-1]
            for si, seg in enumerate(segs):
                t0 = seg * SEGW
                first = (si == 0)
                def load_xss(dt):
                    xs_, xs_te = xss[dt % 3]
                    if not rev:
                        lo = t0 - 3
                        if lo < 0:
                            self.op("dve", lambda E, xs_=xs_: E.memset(
                                xs_[:, 0:3], 0.0), outs=[xs_te])
                            self.dma("sync", xs_[:, 3:3 + SEGW],
                                     self.dram["s_xs_t"][dt * 128:(dt + 1) * 128,
                                                         0:SEGW],
                                     ins=[xs_te], outs=[xs_te])
                        else:
                            self.dma("sync", xs_[:, 0:SEGW + 3],
                                     self.dram["s_xs_t"][dt * 128:(dt + 1) * 128,
                                                         lo:t0 + SEGW],
                                     ins=[xs_te], outs=[xs_te])
                        taps = [(k, k) for k in range(4)]
                    else:
                        hi = t0 + SEGW + 3
                        if hi > T:
                            self.op("dve", lambda E, xs_=xs_: E.memset(
                                xs_[:, SEGW:SEGW + 3], 0.0), outs=[xs_te])
                            self.dma("sync", xs_[:, 0:SEGW],
                                     self.dram["s_xs_t"][dt * 128:(dt + 1) * 128,
                                                         t0:T],
                                     ins=[xs_te], outs=[xs_te])
                        else:
                            self.dma("sync", xs_[:, 0:SEGW + 3],
                                     self.dram["s_xs_t"][dt * 128:(dt + 1) * 128,
                                                         t0:hi],
                                     ins=[xs_te], outs=[xs_te])
                        taps = [(k, 3 - k) for k in range(4)]
                    return taps

                for dt in range(2):
                    load_xss(dt)
                for dt in range(NDT):
                    xs_, xs_te = xss[dt % 3]
                    taps = load_xss(dt) if False else None
                    if not rev:
                        taps = [(k, k) for k in range(4)]
                    else:
                        taps = [(k, 3 - k) for k in range(4)]
                    if dt + 2 < NDT:
                        load_xss(dt + 2)
                    cb_ = 1 + 2 * (dt % 2)
                    fns = []
                    for ki, (k, sh) in enumerate(taps):
                        fns.append(lambda E, k=k, sh=sh, dt=dt, xs_=xs_,
                                   cb_=cb_, st=(ki == 0), sp=(ki == 3):
                                   E.matmul(self.ps[cb_][:, 0:SEGW],
                                            dgb[:, (dt * 4 + k) * 128:
                                                (dt * 4 + k + 1) * 128],
                                            xs_[:, sh:sh + SEGW],
                                            start=st, stop=sp))
                    self.mm_group(fns, ins=[dgb_te, xs_te],
                                  outs=[self.pse[cb_]])
                    self.op("act", lambda E, dt=dt, cb_=cb_: E.activation(
                        xcb[:, dt * SEGW:(dt + 1) * SEGW],
                        self.ps[cb_][:, 0:SEGW],
                        AF.Silu, bias=cbc[:, dt:dt + 1]),
                        ins=[self.pse[cb_], cbc_te], outs=[xcb_te])
                    self.op("pe", lambda E, dt=dt: E.matmul(
                        self.ps[0][0:96, 0:SEGW], xpw[:, dt * 96:(dt + 1) * 96],
                        xcb[:, dt * SEGW:(dt + 1) * SEGW],
                        start=(dt == 0), stop=(dt == NDT - 1)),
                        ins=[xpw_te, xcb_te], outs=[self.pse[0]])
                self.op("act", lambda E: E.copy(dtin[0:64, 0:SEGW],
                                                self.ps[0][0:64, 0:SEGW]),
                        ins=[self.pse[0]], outs=[dtin_te])
                self.op("act", lambda E: E.copy(bcf[64:96, 0:SEGW],
                                                self.ps[0][64:96, 0:SEGW]),
                        ins=[self.pse[0]], outs=[bcf_te])
                self.dma("sync",
                         bflat[0:1, 0:16 * SEGW].rearrange("o (a f) -> o a f", a=16),
                         bcf[64:80, 0:SEGW], ins=[bcf_te], outs=slabA_tes)
                self.dma("sync",
                         cflat[0:1, 0:16 * SEGW].rearrange("o (a f) -> o a f", a=16),
                         bcf[80:96, 0:SEGW], ins=[bcf_te], outs=slabBp[1][1])
                ones_sb, ones_te = self.buf("ones_sb", [1, 128], BF16)
                NCH = 16 * SEGW // 512
                for ch in range(NCH):
                    nsl = list(range(ch * 512 // SEGW,
                                     (ch * 512 + 511) // SEGW + 1))
                    self.op("pe", lambda E, ch=ch: E.matmul(
                        self.ps[2][:, 0:512], ones_sb[0:1, 0:128],
                        bflat[0:1, ch * 512:(ch + 1) * 512],
                        start=True, stop=True),
                        ins=[slabA_tes[n] for n in nsl] + [ones_te],
                        outs=[self.pse[2]])
                    self.op("dve", lambda E, ch=ch: E.tensor_copy(
                        brep[:, ch * 512:(ch + 1) * 512], self.ps[2][:, 0:512]),
                        ins=[self.pse[2]], outs=[brep_te])
                    self.op("pe", lambda E, ch=ch: E.matmul(
                        self.ps[3][:, 0:512], ones_sb[0:1, 0:128],
                        cflat[0:1, ch * 512:(ch + 1) * 512],
                        start=True, stop=True),
                        ins=[slabBp[1][1][n] for n in nsl] + [ones_te],
                        outs=[self.pse[3]])
                    self.op("act", lambda E, ch=ch: E.copy(
                        crep[:, ch * 512:(ch + 1) * 512], self.ps[3][:, 0:512]),
                        ins=[self.pse[3]], outs=[crep_te])
                def emit_hc(dt):
                    slb, slb_tes = slabBp[dt % 2]
                    self.op("dve", lambda E, slb=slb: E.tensor_tensor(
                        slb[:, 0:DST * SEGW], slb[:, 0:DST * SEGW],
                        crep[:, 0:DST * SEGW], op=OP.mult),
                        ins=slb_tes + [crep_te], outs=slb_tes)

                def emit_tail(dt, t0=t0):
                    slb, slb_tes = slabBp[dt % 2]
                    w = DST * SEGW // 2
                    while w >= SEGW:
                        self.op("dve", lambda E, w=w, slb=slb: E.tensor_tensor(
                            slb[:, 0:w], slb[:, 0:w], slb[:, w:2 * w],
                            op=OP.add), ins=slb_tes, outs=slb_tes)
                        w //= 2
                    ystg, ystg_te = ystgp[dt % 2]
                    self.op("dve", lambda E, dt=dt, ystg=ystg, slb=slb:
                            E.scalar_tensor_tensor(
                        ystg[:, 0:SEGW], in0=xcb[:, dt * SEGW:(dt + 1) * SEGW],
                        scalar=dskc[:, dt:dt + 1], in1=slb[:, 0:SEGW],
                        op0=OP.mult, op1=OP.add),
                        ins=[xcb_te, dskc_te] + slb_tes, outs=[ystg_te])
                    self.dma("sync", self.dram[yscr][dt * 128:(dt + 1) * 128,
                                                     t0:t0 + SEGW],
                             ystg[:, 0:SEGW], ins=[ystg_te])

                onec, onec_te = self.buf("onec_sb", [128, 1], F32)
                if onec_te.w is None:
                    self.op("dve", lambda E: E.memset(onec[:, 0:1], 1.0),
                            outs=[onec_te])
                for dt in range(NDT):
                    slb, slb_tes = slabBp[dt % 2]
                    bdt = 1 + (dt % 2)
                    self.op("pe", lambda E, dt=dt, bdt=bdt: E.matmul(
                        self.ps[bdt][:, 0:SEGW],
                        dtw[0:64, dt * 128:(dt + 1) * 128],
                        dtin[0:64, 0:SEGW], start=True, stop=True),
                        ins=[dtw_te, dtin_te], outs=[self.pse[bdt]])
                    spx, spx_te = spxp[dt % 2]
                    spt, spt_te = sptp[dt % 2]
                    self.op("act", lambda E, dt=dt, spx=spx, bdt=bdt:
                            E.activation(spx[:, 0:SEGW], self.ps[bdt][:, 0:SEGW],
                                         AF.Identity, bias=dtbc[:, dt:dt + 1]),
                            ins=[self.pse[bdt], dtbc_te], outs=[spx_te])
                    self.op("act", lambda E, spt=spt, spx=spx: E.activation(
                        spt[:, 0:SEGW], spx[:, 0:SEGW], AF.Abs),
                        ins=[spx_te], outs=[spt_te])
                    self.op("act", lambda E, spt=spt: E.activation(
                        spt[:, 0:SEGW], spt[:, 0:SEGW], AF.Exp, scale=-1.0),
                        ins=[spt_te], outs=[spt_te])
                    self.op("act", lambda E, spt=spt, onec=onec: E.activation(
                        spt[:, 0:SEGW], spt[:, 0:SEGW], AF.Ln,
                        bias=onec[:, 0:1]),
                        ins=[spt_te, onec_te], outs=[spt_te])
                    dts = dttp[dt % 2][0][:, 0:SEGW]
                    self.op("dve", lambda E, dts=dts, spx=spx, spt=spt:
                            E.scalar_tensor_tensor(
                        dts, in0=spx[:, 0:SEGW], scalar=0.0, in1=spt[:, 0:SEGW],
                        op0=OP.max, op1=OP.add),
                        ins=[spx_te, spt_te], outs=[dtt_tes[dt]])
                    self.op("dve", lambda E, dt=dt, dts=dts: E.tensor_tensor(
                        du[:, 0:SEGW], dts, xcb[:, dt * SEGW:(dt + 1) * SEGW],
                        op=OP.mult), ins=[dtt_tes[dt], xcb_te], outs=[du_te])
                    for n in range(DST):
                        self.op("act", lambda E, dt=dt, n=n, dts=dts:
                                E.activation(slabA[:, n * SEGW:(n + 1) * SEGW],
                                             dts, AF.Exp,
                                             scale=asb[:, dt * DST + n:
                                                       dt * DST + n + 1]),
                                ins=[dtt_tes[dt], asb_te], outs=[slabA_tes[n]])
                    dub = du[:, None, 0:SEGW].broadcast_to([128, DST, SEGW])
                    self.op("dve", lambda E, dub=dub, slb=slb: E.tensor_tensor(
                        slb[:, 0:DST * SEGW].rearrange("p (n f) -> p n f", n=DST),
                        dub,
                        brep[:, 0:DST * SEGW].rearrange("p (n f) -> p n f", n=DST),
                        op=OP.mult), ins=[du_te, brep_te], outs=slb_tes)
                    for n in range(DST):
                        sl = slice(n * SEGW, (n + 1) * SEGW)
                        if rev:
                            aps = (slb[:, sl][:, ::-1], slabA[:, sl][:, ::-1],
                                   slb[:, sl][:, ::-1])
                        else:
                            aps = (slb[:, sl], slabA[:, sl], slb[:, sl])
                        self.op("dve", lambda E, n=n, dt=dt, first=first,
                                aps=aps:
                                E.tensor_tensor_scan(
                                    aps[0], aps[1], aps[2],
                                    0.0 if first else
                                    carry[:, dt * DST + n:dt * DST + n + 1],
                                    op0=OP.mult, op1=OP.add),
                                ins=[slabA_tes[n], slb_tes[n], carry_te],
                                outs=[slb_tes[n]])
                    endc = 0 if rev else SEGW - 1
                    hv = slb[:, 0:DST * SEGW].rearrange(
                        "p (n f) -> p n f", n=DST)[:, :, endc:endc + 1]
                    self.op("dve", lambda E, dt=dt, hv=hv:
                            E.tensor_copy(carry[:, dt * DST:(dt + 1) * DST], hv),
                            ins=slb_tes, outs=[carry_te])
                    if dt > 0:
                        emit_tail(dt - 1)
                    emit_hc(dt)
                emit_tail(NDT - 1)
            self.apop()
            self.barrier()
        self.apush()
        TW2 = min(1024, T)
        HW_ = min(512, TW2)
        NH = TW2 // HW_
        yfb = [av(TW2, BF16) for _ in range(NDT)]
        ybb = [av(TW2, BF16) for _ in range(NDT)]
        mo = [av(TW2, BF16) for _ in range(NDT)]
        szbp = [av(TW2, BF16) for _ in range(2)]
        mwb = [av(2 * DI, BF16) for _ in range(2)]
        owb = [av(DI, BF16) for _ in range(2)]
        ostg = [av(TW2, F32) for _ in range(2)]
        yins = [b[1] for b in yfb] + [b[1] for b in ybb]
        for tc in range(T // TW2):
            t0 = tc * TW2
            for k in range(NDT):
                yf, yf_te = yfb[k]
                yb, yb_te = ybb[k]
                self.dma("sync", yf[:, 0:TW2],
                         self.dram["s_yf_t"][k * 128:(k + 1) * 128, t0:t0 + TW2],
                         ins=[yf_te], outs=[yf_te])
                self.dma("sync", yb[:, 0:TW2],
                         self.dram["s_yb_t"][k * 128:(k + 1) * 128, t0:t0 + TW2],
                         ins=[yb_te], outs=[yb_te])
            for m in range(NDT):
                pb = 2 * (m % 2)
                wt, wt_te = mwb[m % 2]
                self.dma("sync", wt[:, 0:2 * DI],
                         self.dram["s_mw"][m * 128:(m + 1) * 128, :],
                         ins=[wt_te, self.scr_te["s_mw"]], outs=[wt_te])
                szb, szb_te = szbp[m % 2]
                self.dma("pool", szb[:, 0:TW2],
                         self.dram["s_sz_t"][m * 128:(m + 1) * 128, t0:t0 + TW2],
                         ins=[szb_te], outs=[szb_te])
                fns = []
                for k in range(2 * NDT):
                    ysrc, _ = yfb[k] if k < NDT else ybb[k - NDT]
                    for hh in range(NH):
                        fns.append(lambda E, k=k, hh=hh, pb=pb, wt=wt,
                                   ysrc=ysrc:
                                   E.matmul(self.ps[pb + hh][:, 0:HW_],
                                            wt[:, k * 128:(k + 1) * 128],
                                            ysrc[:, hh * HW_:(hh + 1) * HW_],
                                            start=(k == 0),
                                            stop=(k == 2 * NDT - 1)))
                self.mm_group(fns, ins=[wt_te] + yins,
                              outs=[self.pse[pb + hh2] for hh2 in range(NH)])
                mob, mob_te = mo[m]
                for hh in range(NH):
                    self.op("dve", lambda E, hh=hh, pb=pb, mob=mob, szb=szb:
                            E.tensor_tensor(mob[:, hh * HW_:(hh + 1) * HW_],
                                            self.ps[pb + hh][:, 0:HW_],
                                            szb[:, hh * HW_:(hh + 1) * HW_],
                                            op=OP.mult),
                            ins=[self.pse[pb + hh], szb_te], outs=[mob_te])
            moins = [b[1] for b in mo]
            for fo in range(8):
                wt, wt_te = owb[fo % 2]
                self.dma("sync", wt[:, 0:DI],
                         self.dram["s_ow"][fo * 128:(fo + 1) * 128, :],
                         ins=[wt_te, self.scr_te["s_ow"]], outs=[wt_te])
                fns = []
                for k in range(NDT):
                    mob, _ = mo[k]
                    for hh in range(NH):
                        fns.append(lambda E, k=k, hh=hh, wt=wt, mob=mob:
                                   E.matmul(self.ps[4 + hh][:, 0:HW_],
                                            wt[:, k * 128:(k + 1) * 128],
                                            mob[:, hh * HW_:(hh + 1) * HW_],
                                            start=(k == 0), stop=(k == NDT - 1)))
                self.mm_group(fns, ins=[wt_te] + moins,
                              outs=[self.pse[4 + hh2] for hh2 in range(NH)])
                og, og_te = ostg[fo % 2]
                for hh in range(NH):
                    self.op("act", lambda E, hh=hh, og=og: E.copy(
                        og[:, hh * HW_:(hh + 1) * HW_],
                        self.ps[4 + hh][:, 0:HW_]),
                        ins=[self.pse[4 + hh]], outs=[og_te])
                for s128 in range(TW2 // 128):
                    tt = (t0 + s128 * 128) // 128
                    rsl = self.r[:, tt * C + fo * 128: tt * C + (fo + 1) * 128]
                    self.transpose_addr(og[:, s128 * 128:(s128 + 1) * 128],
                                        og_te, rsl, self.rT[tt],
                                        pbank=6 + (fo % 2))
        self.apop()
        self.apop()
        self.barrier()


def host_constants(T):
    inv_freq = (1.0 / 10000.0 ** (np.arange(0, LAT, 2, dtype=np.float32) / LAT))
    t = np.arange(T, dtype=np.float32)
    freqs = np.outer(t, inv_freq.astype(np.float32)).astype(np.float32)
    cos = np.repeat(np.cos(freqs), 2, axis=-1)[:, :LAT].astype(np.float32)
    sin = np.repeat(np.sin(freqs), 2, axis=-1)[:, :LAT].astype(np.float32)
    cos4 = np.ascontiguousarray(np.tile(cos.T, (4, 1)))
    sin4 = np.ascontiguousarray(np.tile(sin.T, (4, 1)))
    R = np.zeros((LAT, LAT), np.float32)
    for i in range(0, LAT, 2):
        R[i, i + 1] = -1.0
        R[i + 1, i] = 1.0
    R4 = np.kron(np.eye(4, dtype=np.float32), R.T)
    ident = np.eye(128, dtype=np.float32)
    QCW = min(512, T)
    nm = min(4, max(1, QCW // 128))
    masks = np.zeros((nm * 128, QCW), np.float32)
    for mi in range(nm):
        off = mi * 128
        kl = np.arange(128)[:, None]
        ql = np.arange(QCW)[None, :]
        masks[mi * 128:(mi + 1) * 128] = (ql >= kl + off).astype(np.float32)
    return {"ident": ident, "identb": ident, "R4_t": R4, "cos4": cos4,
            "sin4": sin4, "ones_row": np.ones((1, 128), np.float32),
            "ones_m": np.ones((128, LAT), np.float32),
            "masks": masks}


def core_inputs(ins, T=2048):
    import ml_dtypes
    g = lambda k: np.ascontiguousarray(np.asarray(ins[k], dtype=np.float32))
    col = lambda k: np.ascontiguousarray(g(k).reshape(-1, 1))
    wqk_t = g("L0_attn_a_wqk_w").T
    m = {
        "c_attn_w": g("L0_attn_a_c_attn_w"),
        "c_attn_b": col("L0_attn_a_c_attn_b"),
        "q2l_w_t": np.ascontiguousarray(g("L0_attn_a_q2l_w").T),
        "kv2l_w_t": np.ascontiguousarray(g("L0_attn_a_kv2l_w").T),
        "q2l_b": col("L0_attn_a_q2l_b"),
        "kv2l_b": col("L0_attn_a_kv2l_b"),
        "kv2l_b_row": np.ascontiguousarray(
            g("L0_attn_a_kv2l_b").reshape(1, -1)),
        "l2o_w_t": np.ascontiguousarray(g("L0_attn_a_l2o_w").T),
        "l2o_b": col("L0_attn_a_l2o_b"),
        "wqk_w_t4": np.ascontiguousarray(
            np.kron(np.eye(4, dtype=np.float32), wqk_t)),
        "wqk_b4": np.ascontiguousarray(
            np.tile(g("L0_attn_a_wqk_b"), 4).reshape(-1, 1)),
        "cproj_w_t": np.ascontiguousarray(g("L0_attn_a_cproj_w").T),
        "cproj_b": col("L0_attn_a_cproj_b"),
        "ffn_wg": g("L0_ffn_w_gate"),
        "ffn_wu": g("L0_ffn_w_up"),
        "ffn_wd": g("L0_ffn_w_down"),
        "bb_gamma": col("L0_attn_b_gamma"),
        "bb_beta": col("L0_attn_b_beta"),
        "bb_in_w": g("L0_attn_b_in_w"),
        "bb_merge_w": g("L0_attn_b_merge_w"),
        "bb_out_w": g("L0_attn_b_out_w"),
    }
    for sfx in ("", "_b"):
        m["bb_conv_w" + sfx] = g("L0_attn_b_conv_w" + sfx)
        m["bb_conv_b" + sfx] = col("L0_attn_b_conv_b" + sfx)
        m["bb_xproj_w" + sfx] = g("L0_attn_b_xproj_w" + sfx)
        m["bb_dt_w" + sfx] = g("L0_attn_b_dt_w" + sfx)
        m["bb_dt_b" + sfx] = col("L0_attn_b_dt_b" + sfx)
        m["bb_A_log" + sfx] = g("L0_attn_b_A_log" + sfx)
        m["bb_D" + sfx] = col("L0_attn_b_D_skip" + sfx)
    cst = host_constants(T)
    cst["identb"] = cst["identb"].astype(ml_dtypes.bfloat16)
    cst["masks"] = cst["masks"].astype(ml_dtypes.bfloat16)
    m.update(cst)
    return m


PARTS = ("a", "f1", "b", "f2")


def kernel(**inputs):
    x = np.ascontiguousarray(np.asarray(inputs["L0_attn_a_x"], dtype=np.float32))
    B, T, _ = x.shape
    kb = KB(T=T, parts=PARTS)
    nc = kb.build()
    base = core_inputs(inputs, T)
    in_maps = []
    for i in range(B):
        mi = dict(base)
        mi["x"] = x[i]
        in_maps.append(mi)
    res = run_bass_kernel_spmd(nc, in_maps, core_ids=list(range(B)))
    out = np.stack([np.asarray(r["y"]) for r in res.results], axis=0)
    return out.astype(np.float32)
